# version 05
# baseline (speedup 1.0000x reference)
import numpy as np
import ml_dtypes
import contextlib
import concourse.bass as bass
import concourse.mybir as mybir
from concourse.bass_utils import run_bass_kernel_spmd

F32 = mybir.dt.float32
BF16 = mybir.dt.bfloat16
I32 = mybir.dt.int32
AF = mybir.ActivationFunctionType
ALU = mybir.AluOpType
BF = ml_dtypes.bfloat16
AX = mybir.AxisListType.X

B, T, D = 4, 4096, 1024
TOWN = 2048
NCH = 8
HOUT, HD = 16, 64
VECW = 8448
WSLAB = 2432
HALO = 514
YW = 2 + HALO + TOWN
EPS = 1e-5
HID, HIDP = 2730, 2816
NAC = HIDP // 128
TRIM_W = [128, 256, 384, 512, 512, 384, 256, 128]
TRIM_Q0 = [0, 0, 0, 0, 0, 128, 256, 384]
TRIM_OFF = [0, 128, 384, 768, 1280, 1792, 2176, 2432]
PV_ORDER = [4, 3, 2, 1, 0, 5, 6, 7]

DBG_SPECS = {
    "xbf": ([NCH, 128, T], BF16),
    "ksb": ([128, T], BF16),
    "qsb": ([128, TOWN], BF16),
    "vpr": ([128, 32 * 130], BF16),
    "u8": ([NCH, 128, TOWN], BF16),
    "xsand": ([NCH, 128, TOWN], F32),
    "ebv": ([HOUT, VECW], BF16),
    "ebc": ([128, 24], F32),
    "hh": ([1, 128, 11264], BF16),
    "hh0": ([1, 128, 11264], BF16),
    "ebs": ([128, 16], BF16),
    "ebe": ([128, 2048], BF16),
    "ybf": ([NCH, 128, YW], BF16),
    "kf": ([4, 128, 2560], BF16),
    "qf": ([4, 128, 2048], BF16),
    "osb": ([4, 128, 512], BF16),
    "x2mid": ([NCH, 128, TOWN], F32),
    "x1": ([NCH, 128, TOWN], F32),
    "xn": ([NCH, 128, TOWN], BF16),
}

CS_IDF, CS_KOFF, CS_KAMP, CS_KSHP = 0, 128, 130, 132
CS_KEEP, CS_NEGS, CS_QSC, CS_KSC = 134, 136, 140, 144
CS_B1, CS_B2, CS_LNG, CS_LNB = 148, 240, 260, 280
CB_ID, CB_ONES, CB_PERM, CB_SELH, CB_SELAMP, CB_BLK2, CB_SEL2 = 0, 128, 256, 384, 400, 432, 440

W0_REL_F = 0
W0_STG_F = VECW
W0_SQ_F = VECW + 2 * 2048
W0_BCAST = 2 * (VECW + 3 * 2048)
W0_E = W0_BCAST + 2 * 2048
W0_EB = W0_E + 2048
W0_LN = W0_EB + 2048
A_KSB = 0
A_QSB = 2 * T
A_VPR = A_QSB + 2 * TOWN
A_WSL = A_VPR + 2 * 4160
A_WST = A_WSL + 8 * WSLAB
A_PT = A_WST + 2 * 3072
A_VST = A_PT + 2048
A_RZB = A_VST + 512
A_UST = A_RZB + 1024
E_U = 0
E_SQ = 2 * 2048
E_RZB = 3 * 2048
E_AB_F = 8192
E_BB_F = 8192 + 2048
E_XT_F = 8192 + 2 * 2048
E_CST_F = 8192 + 2 * 2048 + 2048
L_Y = 0
L_KF = 22528
L_QF = L_KF + 4 * 2560
L_XN = L_KF
L_VPR = L_QF + 4 * 2048
L_OSB = L_VPR + 2600
L_AB = L_OSB + 2048
L_WQS = L_AB + 4096
L_MSK = L_WQS + 2 * 1536
L_W1S = L_MSK + 2 * 2560
L_W2S = L_W1S + 2 * 768
L_WOS = L_W2S + 2 * 1024
L_PST = L_WOS + 4 * 1024
L_RZB = L_PST + 1024
L_SQQ = L_RZB + 512
L_QRW = L_SQQ + 512
L_F32 = 46000
L_AB2_F = 23000
F_SQ = 0
F_CST = 2048
F_AB_F = 4096 // 2 + 0
F_XS_F = 2048 + 2 * 2048
F_OUT_F = 2048 + 4 * 2048


class Bld:
    def __init__(self, nc, block, SM, ten, dbg):
        self.nc, self.block, self.SM, self.t, self.dbg = nc, block, SM, ten, dbg
        self.C = {k: 0 for k in SM}
        self.mk = {}
        self.barn = 0

    def inc(self, name, k=1):
        self.C[name] += k
        return self.C[name]

    def bar(self):
        self.barn += 1
        tgt = 4 * self.barn
        SM = self.SM
        for engsel in ["scalar", "vector", "gpsimd", "sync"]:
            def body(e, drn=(engsel != "sync")):
                if drn:
                    e.drain()
                e.nop().then_inc(SM["sS"], 1)
                e.wait_ge(SM["sS"], tgt)
            getattr(self.block, engsel)(body)
        self.block.tensor(lambda e: e.wait_ge(SM["sS"], tgt))
        self.C["sS"] = tgt

    def pinc(self, L, sem="cP", n=1):
        f = L.pop()
        s = self.SM[sem]
        L.append(lambda e, f=f: f(e).then_inc(s, n))

    def dnop(self, L, sem, n=1):
        s = self.SM[sem]
        L.append(lambda e: e.drain())
        L.append(lambda e: e.nop().then_inc(s, n))

    def xbf(self, c):
        return self.t["bigA"][:].bitcast(BF16)[:, c * T:(c + 1) * T]

    def Rown(self, c):
        return self.t["bigA"][:, c * 2052 + 2:c * 2052 + 2 + TOWN]

    def Rsh(self, c):
        return self.t["bigA"][:, c * 2052 + 1:c * 2052 + 1 + TOWN]

    def Rm1(self, c):
        return self.t["bigA"][:, c * 2052 + 1:c * 2052 + 2]

    def Rcol(self, c, a, n):
        return self.t["bigA"][:, c * 2052 + a:c * 2052 + a + n]

    def rhc(self, c):
        return self.t["rh"][:, c * 516:(c + 1) * 516]

    def wf(self, a, n):
        return self.t["WS"][:].bitcast(F32)[:, a:a + n]

    def wb(self, a, n):
        return self.t["WS"][:, a:a + n]


def build_nc(stop_phase=99, dbg=()):
    nc = bass.Bass("TRN2", target_bir_lowering=False, debug=False)
    dbg = set(dbg)
    dram = {}
    for nm, shp, dt_ in [
        ("x_own", [TOWN, D], F32), ("x_oth", [TOWN, D], F32),
        ("w_in", [3 * D, D], F32),
        ("koff", [128, 2], F32), ("kamp", [128, 2], F32), ("kshp", [128, 2], F32),
        ("rel", [1, VECW], F32), ("selh", [128, 16], BF16),
        ("ident_bf", [128, 128], BF16), ("ident_f", [128, 128], F32),
        ("ones_bf", [128, 128], BF16), ("permsw", [128, 128], BF16),
        ("blk2", [128, 2], BF16), ("sel2", [2, 128], BF16),
        ("idxW", [128, 128], I32),
        ("wqkv_in", [2, D, 1536], F32), ("wo_in", [2, 512, D], F32),
        ("w1p_in", [2, D, 2 * HIDP], F32), ("w2p_in", [2, HIDP, D], F32),
        ("b1p", [128, 4 * NAC], F32), ("b2p", [128, 2 * NCH], F32),
        ("lngp", [128, 2 * NCH], F32), ("lnbp", [128, 2 * NCH], F32),
        ("qscp", [128, 2], F32), ("kscp", [128, 2], F32),
        ("rotcq", [128, 2560], BF16), ("rotsq", [128, 2560], BF16),
        ("rotck", [128, 2560], BF16), ("rotsk", [128, 2560], BF16),
        ("mgen", [128, 2560], BF16), ("mw0", [128, 2560], BF16),
        ("keepc", [128, 1], F32),
        ("idxEx", [128, 8], I32), ("ridxEx", [128, 8], I32),
        ("idxC", [128, 1], I32), ("ridxC", [128, 1], I32),
        ("fwidx", [2, 1], I32), ("fridx", [2, 1], I32),
    ]:
        dram[nm] = nc.dram_tensor(nm, shp, dt_, kind="ExternalInput").ap()

    y_out = nc.dram_tensor("y_out", [TOWN, D], F32, kind="ExternalOutput").ap()
    dbg_out = {}
    for name in dbg:
        shp, dt_ = DBG_SPECS[name]
        dbg_out[name] = nc.dram_tensor("dbg_" + name, shp, dt_, kind="ExternalOutput").ap()

    scr = {}
    for nm, shp, dt_ in [
        ("w_in_bf", [3 * D, D], BF16),
        ("wqkv_bf", [2 * D, 1536], BF16),
        ("wo_bf", [2 * 512, D], BF16),
        ("w1_bf", [2 * D, 2 * HIDP], BF16),
        ("w2_bf", [2 * HIDP, D], BF16),
        ("ebvec", [HOUT * VECW], BF16),
        ("u_sc", [NCH, 128, TOWN], BF16),
        ("xs_sc", [NCH, 128, TOWN], F32),
    ]:
        scr[nm] = nc.dram_tensor(nm, shp, dt_, kind="Internal").ap()
    for nm, shp in [
        ("shE1", [2 * NCH * 128, HALO]), ("shE2", [2 * NCH * 128, HALO]),
        ("shC1", [2 * 128, NCH]), ("shC2", [2 * 128, NCH]),
    ]:
        scr[nm] = nc.dram_tensor(nm, shp, F32, kind="Internal", addr_space="Shared").ap()
    scr["shFlag"] = nc.dram_tensor("shFlag", [4, 8], I32, kind="Internal",
                                   addr_space="Shared").ap()

    with (
        nc.sbuf_tensor("bigA", [128, 16704], F32) as bigA,
        nc.sbuf_tensor("WS", [128, 55040], BF16) as WS,
        nc.sbuf_tensor("rh", [128, NCH * 516], F32) as rh,
        nc.sbuf_tensor("cst", [128, 352], F32) as cst,
        nc.sbuf_tensor("cstb", [128, 1024], BF16) as cstb,
        nc.sbuf_tensor("row", [128, 64], F32) as row,
        nc.sbuf_tensor("flg", [128, 8], I32) as flg,
        nc.sbuf_tensor("flgin", [2, 8], I32) as flgin,
        nc.sbuf_tensor("idxs", [128, 160], I32) as idxs,
        nc.psum_tensor("pw", [128, 2048], F32) as pw,
        nc.psum_tensor("pb0", [128, 512], F32) as pb0,
        nc.psum_tensor("pb1", [128, 512], F32) as pb1,
        nc.psum_tensor("pb2", [128, 512], F32) as pb2,
        nc.psum_tensor("pt", [128, 1024], BF16) as pt,
        nc.Block() as block,
        contextlib.ExitStack() as stack,
    ):
        SM = {}
        for s in ["sW", "sX", "sG", "sZ", "sZ1", "sZ2", "sZ3", "sZ4",
                  "cP", "cA", "cV", "cO", "sS"]:
            SM[s] = stack.enter_context(nc.semaphore(s))
        ten = dict(bigA=bigA, WS=WS, rh=rh, cst=cst, cstb=cstb, row=row,
                   flg=flg, flgin=flgin, idxs=idxs, pw=pw, pb0=pb0, pb1=pb1,
                   pb2=pb2, pt=pt, **dram, **scr)
        b = Bld(nc, block, SM, ten, dbg)
        ph_consts(b)
        if stop_phase >= 0.5:
            ph_wcast(b)
        if stop_phase >= 0.9:
            ph_ebvec(b, dbg_out)
        b.bar()
        if stop_phase >= 2:
            ph_entry(b, dbg_out)
            b.bar()
        if stop_phase >= 3:
            ph_attn(b, dbg_out)
            b.bar()
        if stop_phase >= 4:
            ph_epilogue1(b, dbg_out)
            b.bar()
        if stop_phase >= 5:
            ph_pre_ffn(b)
            b.bar()
            ph_layer(b, 0, dbg_out)
            b.bar()
        if stop_phase >= 6:
            ph_e2(b)
            b.bar()
            ph_layer(b, 1, dbg_out)
            b.bar()
        if stop_phase >= 6.2:
            sub = 3 if stop_phase >= 7 else (1 if stop_phase >= 6.5 else 0)
            ph_final(b, y_out, sub=sub)
        b.bar()
    return nc


def ph_consts(b):
    block, SM, t, C = b.block, b.SM, b.t, b.C

    @block.sync
    def _(sp):
        pairs = [
            (t["cstb"][:, CB_ID:CB_ID + 128], t["ident_bf"]),
            (t["cstb"][:, CB_ONES:CB_ONES + 128], t["ones_bf"]),
            (t["cstb"][:, CB_PERM:CB_PERM + 128], t["permsw"]),
            (t["cstb"][:, CB_SELH:CB_SELH + 16], t["selh"]),
            (t["cstb"][:, CB_BLK2:CB_BLK2 + 2], t["blk2"]),
            (t["cstb"][0:2, CB_SEL2:CB_SEL2 + 128], t["sel2"]),
            (t["cst"][:, CS_IDF:CS_IDF + 128], t["ident_f"]),
            (t["cst"][:, CS_KOFF:CS_KOFF + 2], t["koff"]),
            (t["cst"][:, CS_KAMP:CS_KAMP + 2], t["kamp"]),
            (t["cst"][:, CS_KSHP:CS_KSHP + 2], t["kshp"]),
            (t["cst"][:, CS_KEEP:CS_KEEP + 1], t["keepc"]),
            (t["cst"][:, CS_QSC:CS_QSC + 2], t["qscp"]),
            (t["cst"][:, CS_KSC:CS_KSC + 2], t["kscp"]),
            (t["cst"][:, CS_B1:CS_B1 + 4 * NAC], t["b1p"]),
            (t["cst"][:, CS_B2:CS_B2 + 2 * NCH], t["b2p"]),
            (t["cst"][:, CS_LNG:CS_LNG + 2 * NCH], t["lngp"]),
            (t["cst"][:, CS_LNB:CS_LNB + 2 * NCH], t["lnbp"]),
            (t["idxs"][:, 0:128], t["idxW"]),
            (t["idxs"][:, 128:136], t["idxEx"]),
            (t["idxs"][:, 136:144], t["ridxEx"]),
            (t["idxs"][:, 144:145], t["idxC"]),
            (t["idxs"][:, 145:146], t["ridxC"]),
            (t["idxs"][0:2, 146:147], t["fwidx"]),
            (t["idxs"][0:2, 147:148], t["fridx"]),
        ]
        for dst, src in pairs:
            sp.dma_start(out=dst, in_=src).then_inc(SM["sW"], 16)
        sp.dma_start(out=b.wf(W0_REL_F, VECW),
                     in_=t["rel"].partition_broadcast(128)).then_inc(SM["sW"], 16)
    b.inc("sW", 25 * 16)
    b.mk["consts"] = C["sW"]

    @block.vector
    def _(v):
        for e in range(4):
            v.memset(t["flg"][32 * e:32 * e + 2, :], e + 1)
        v.memset(t["flgin"][:], 0)
        v.memset(t["rh"][:], 0.0)
        v.memset(t["row"][:], 0.0)
        v.drain()
        v.memset(t["row"][:, 10:11], EPS)
        v.wait_ge(SM["sW"], b.mk["consts"])
        v.tensor_scalar(out=t["row"][:, 16:18],
                        in0=t["cst"][:, CS_KSHP:CS_KSHP + 2], scalar1=-1.0,
                        scalar2=None, op0=ALU.mult)
        v.drain()
        v.tensor_tensor(out=t["cst"][:, CS_NEGS:CS_NEGS + 2],
                        in0=t["row"][:, 16:18],
                        in1=t["cst"][:, CS_KSHP:CS_KSHP + 2], op=ALU.min)
        for ch in range(2):
            v.tensor_scalar(out=t["cstb"][:, CB_SELAMP + ch * 8:CB_SELAMP + ch * 8 + 8],
                            in0=t["cstb"][:, CB_SELH + ch * 8:CB_SELH + ch * 8 + 8],
                            scalar1=t["cst"][:, CS_KAMP + ch:CS_KAMP + ch + 1],
                            scalar2=None, op0=ALU.mult)
        v.drain()
        v.nop().then_inc(SM["cV"], 1)
    b.inc("cV")
    b.mk["ph0v"] = C["cV"]


def ph_wcast(b):
    block, SM, t, C = b.block, b.SM, b.t, b.C
    jobs = []
    for i in range(24):
        jobs.append((t["w_in"][i * 128:(i + 1) * 128, :],
                     t["w_in_bf"][i * 128:(i + 1) * 128, :], 1024))
    for l in range(2):
        for i in range(8):
            jobs.append((t["wqkv_in"][l, i * 128:(i + 1) * 128, :],
                         t["wqkv_bf"][l * D + i * 128:l * D + (i + 1) * 128, :], 1536))
        for i in range(4):
            jobs.append((t["wo_in"][l, i * 128:(i + 1) * 128, :],
                         t["wo_bf"][l * 512 + i * 128:l * 512 + (i + 1) * 128, :], 1024))
        for i in range(8):
            for hq in range(4):
                jobs.append((t["w1p_in"][l, i * 128:(i + 1) * 128, hq * 1408:(hq + 1) * 1408],
                             t["w1_bf"][l * D + i * 128:l * D + (i + 1) * 128,
                                        hq * 1408:(hq + 1) * 1408], 1408))
        for i in range(22):
            jobs.append((t["w2p_in"][l, i * 128:(i + 1) * 128, :],
                         t["w2_bf"][l * HIDP + i * 128:l * HIDP + (i + 1) * 128, :], 1024))
    nj = len(jobs)
    m_sX, m_cA, m_sG = C["sX"], C["cA"], C["sG"]

    @block.sync
    def _(sp):
        for i, (src, dst, w) in enumerate(jobs):
            if i >= 2:
                sp.wait_ge(SM["cA"], m_cA + i - 1)
            sp.dma_start(out=b.wf(W0_STG_F + (i % 2) * 2048, w), in_=src).then_inc(SM["sX"], 16)
    b.inc("sX", nj * 16)

    @block.scalar
    def _(a):
        for i in range(nj):
            w = jobs[i][2]
            a.wait_ge(SM["sX"], m_sX + (i + 1) * 16)
            if i >= 2:
                a.wait_ge(SM["sG"], m_sG + (i - 1) * 16)
            a.activation(out=b.wb(W0_BCAST + (i % 2) * 2048, w),
                         in_=b.wf(W0_STG_F + (i % 2) * 2048, w),
                         func=AF.Copy).then_inc(SM["cA"], 1)
    b.inc("cA", nj)

    @block.gpsimd
    def _(g):
        for i, (src, dst, w) in enumerate(jobs):
            g.wait_ge(SM["cA"], m_cA + i + 1)
            g.dma_start(out=dst, in_=b.wb(W0_BCAST + (i % 2) * 2048, w)).then_inc(SM["sG"], 16)
    b.inc("sG", nj * 16)
    b.mk["wcast"] = C["sG"]


def ph_ebvec(b, dbg_out):
    block, SM, t, C = b.block, b.SM, b.t, b.C
    blks = [(i * 2048, min(2048, VECW - i * 2048)) for i in range((VECW + 2047) // 2048)]
    m_cA, m_cP, m_sG = C["cA"], C["cP"], C["sG"]
    m_cO_eb = C["cO"]

    @block.scalar
    def _(a):
        a.wait_ge(SM["cV"], b.mk["ph0v"])
        a.wait_ge(SM["sG"], b.mk["wcast"])
        for i, (off, w) in enumerate(blks):
            for ch in range(2):
                a.activation(out=b.wf(W0_SQ_F, w), in_=b.wf(W0_REL_F + off, w),
                             func=AF.Square,
                             bias=t["cst"][:, CS_KOFF + ch:CS_KOFF + ch + 1], scale=-1.0)
                a.drain()
                a.activation(out=b.wb(W0_E, w), in_=b.wf(W0_SQ_F, w), func=AF.Exp,
                             scale=t["cst"][:, CS_NEGS + ch:CS_NEGS + ch + 1]).then_inc(SM["cA"], 1)
                a.wait_ge(SM["cP"], m_cP + i * 2 + ch + 1)
            if i >= 1:
                a.wait_ge(SM["cO"], m_cO_eb + i)
            a.activation(out=b.wb(W0_EB, w)[0:8, :], in_=t["pw"][0:8, 0:w],
                         func=AF.Copy)
            a.activation(out=b.wb(W0_EB, w)[32:40, :], in_=t["pw"][32:40, 0:w],
                         func=AF.Copy)
            a.drain()
            a.nop().then_inc(SM["cA"], 1)
    b.inc("cA", len(blks) * 3)

    @block.tensor
    def _(tt):
        for i, (off, w) in enumerate(blks):
            if i > 0:
                tt.wait_ge(SM["cA"], m_cA + i * 3)
            for ch in range(2):
                tt.wait_ge(SM["cA"], m_cA + i * 3 + ch + 1)
                base = 0 if ch == 0 else 32
                nj = (w + 511) // 512
                for j in range(nj):
                    wj = min(512, w - j * 512)
                    m = tt.matmul(t["pw"][base:base + 8, j * 512:j * 512 + wj],
                                  t["cstb"][:, CB_SELAMP + ch * 8:CB_SELAMP + ch * 8 + 8],
                                  b.wb(W0_E + j * 512, wj),
                                  start=True, stop=True, tile_position=(0, base))
                m.then_inc(SM["cP"], 1)
    b.inc("cP", len(blks) * 2)

    ndma = 2 if "ebv" in b.dbg else 1

    @block.gpsimd
    def _(g):
        ebh = t["ebvec"].rearrange("(h u) -> h u", h=HOUT)
        for i, (off, w) in enumerate(blks):
            g.wait_ge(SM["cA"], m_cA + i * 3 + 3)
            g.dma_start(out=ebh[0:8, off:off + w],
                        in_=b.wb(W0_EB, w)[0:8, :]).then_inc(SM["sG"], 16)
            g.dma_start(out=ebh[8:16, off:off + w],
                        in_=b.wb(W0_EB, w)[32:40, :]).then_inc(SM["sG"], 16)
            if "ebv" in b.dbg:
                g.dma_start(out=dbg_out["ebv"][0:8, off:off + w],
                            in_=b.wb(W0_EB, w)[0:8, :]).then_inc(SM["sG"], 16)
                g.dma_start(out=dbg_out["ebv"][8:16, off:off + w],
                            in_=b.wb(W0_EB, w)[32:40, :]).then_inc(SM["sG"], 16)
            g.wait_ge(SM["sG"], m_sG + (i + 1) * 32 * ndma)
            g.nop().then_inc(SM["cO"], 1)
        if "ebc" in b.dbg:
            g.dma_start(out=dbg_out["ebc"], in_=t["cst"][:, 128:152]).then_inc(SM["sG"], 16)
            g.dma_start(out=dbg_out["ebs"],
                        in_=t["cstb"][:, CB_SELAMP:CB_SELAMP + 16]).then_inc(SM["sG"], 16)
            g.dma_start(out=dbg_out["ebe"], in_=b.wb(W0_E, 2048)).then_inc(SM["sG"], 16)
    b.inc("sG", len(blks) * 32 * ndma + (48 if "ebc" in b.dbg else 0))
    b.inc("cO", len(blks))
    b.mk["ebvec"] = C["sG"]


def ph_entry(b, dbg_out):
    block, SM, t, C = b.block, b.SM, b.t, b.C
    m_sX, m_cA, m_cV, m_cP, m_cO = C["sX"], C["cA"], C["cV"], C["cP"], C["cO"]

    @block.sync
    def _(sp):
        for i in range(32):
            src = t["x_own"] if i < 16 else t["x_oth"]
            ti = i % 16
            if i >= 2:
                sp.wait_ge(SM["cA"], m_cA + 2 * (i - 1))
            sp.dma_start(out=b.wf(W0_STG_F + (i % 2) * 1024, 1024),
                         in_=src[ti * 128:(ti + 1) * 128, :]).then_inc(SM["sX"], 16)
    b.inc("sX", 32 * 16)

    @block.vector
    def _(v):
        for i in range(33):
            if 1 <= i:
                v.wait_ge(SM["cP"], m_cP + i)
                for c in range(8):
                    v.tensor_copy(out=b.xbf(c)[:, (i - 1) * 128:i * 128],
                                  in_=t["pt"][:, c * 128:(c + 1) * 128])
                v.drain()
                v.nop().then_inc(SM["cO"], 1)
            if i < 32:
                xt = b.wf(W0_STG_F + (i % 2) * 1024, 1024)
                v.wait_ge(SM["sX"], m_sX + (i + 1) * 16)
                v.reduce_sum(out=t["row"][:, 0:1], in_=xt, axis=AX)
                v.tensor_mul(out=b.wf(W0_SQ_F, 1024), in0=xt, in1=xt)
                v.drain()
                v.reduce_sum(out=t["row"][:, 1:2], in_=b.wf(W0_SQ_F, 1024), axis=AX)
                v.tensor_scalar(out=t["row"][:, 2:3], in0=t["row"][:, 0:1],
                                scalar1=1.0 / D, scalar2=None, op0=ALU.mult)
                v.drain()
                v.tensor_scalar(out=t["row"][:, 3:4], in0=t["row"][:, 1:2],
                                scalar1=1.0 / D, scalar2=None, op0=ALU.mult)
                v.tensor_mul(out=t["row"][:, 4:5], in0=t["row"][:, 2:3], in1=t["row"][:, 2:3])
                v.drain()
                v.tensor_sub(out=t["row"][:, 5:6], in0=t["row"][:, 3:4],
                             in1=t["row"][:, 4:5]).then_inc(SM["cV"], 1)
                v.wait_ge(SM["cA"], m_cA + 2 * i + 1)
                v.tensor_mul(out=t["row"][:, 8:9], in0=t["row"][:, 2:3], in1=t["row"][:, 7:8])
                v.drain()
                v.tensor_scalar(out=t["row"][:, 9:10], in0=t["row"][:, 8:9],
                                scalar1=-1.0, scalar2=None, op0=ALU.mult).then_inc(SM["cV"], 1)
    b.inc("cV", 64)
    b.inc("cO", 32)

    @block.scalar
    def _(a):
        for i in range(32):
            xt = b.wf(W0_STG_F + (i % 2) * 1024, 1024)
            a.wait_ge(SM["cV"], m_cV + 2 * i + 1)
            a.activation(out=t["row"][:, 6:7], in_=t["row"][:, 5:6], func=AF.Ln,
                         bias=t["row"][:, 10:11])
            a.drain()
            a.activation(out=t["row"][:, 7:8], in_=t["row"][:, 6:7], func=AF.Exp,
                         scale=-0.5).then_inc(SM["cA"], 1)
            a.wait_ge(SM["cV"], m_cV + 2 * i + 2)
            if i >= 1:
                a.wait_ge(SM["cP"], m_cP + i)
            a.activation(out=b.wb(W0_LN, 1024), in_=xt, func=AF.Identity,
                         bias=t["row"][:, 9:10], scale=t["row"][:, 7:8]).then_inc(SM["cA"], 1)
    b.inc("cA", 64)

    @block.tensor
    def _(tt):
        for i in range(32):
            tt.wait_ge(SM["cA"], m_cA + 2 * i + 2)
            if i >= 1:
                tt.wait_ge(SM["cO"], m_cO + i)
            for c in range(8):
                m = tt.transpose(t["pt"][:, c * 128:(c + 1) * 128],
                                 b.wb(W0_LN + c * 128, 128),
                                 t["cstb"][:, CB_ID:CB_ID + 128])
            m.then_inc(SM["cP"], 1)
    b.inc("cP", 32)
    b.mk["entry_cO"] = C["cO"]

    if "xbf" in b.dbg:
        @block.gpsimd
        def _(g):
            g.wait_ge(SM["cO"], b.mk["entry_cO"])
            for c in range(8):
                g.dma_start(out=dbg_out["xbf"][c], in_=b.xbf(c)).then_inc(SM["sG"], 16)
        b.inc("sG", 8 * 16)


A_RZF = (A_UST + 2 * 1024) // 2


def ph_attn(b, dbg_out):
    block, SM, t, C, mk = b.block, b.SM, b.t, b.C, b.mk
    LP, LA, LV, LG, LS = [], [], [], [], []

    def KS(s):
        return b.wb(A_KSB + s * T, T)

    def QS(s):
        return b.wb(A_QSB + s * TOWN, TOWN)

    def VPo(s):
        return A_VPR + s * 4160

    def WSL(s):
        return A_WSL + s * 4 * WSLAB

    def WST(s):
        return b.wb(A_WST + s * 3072, 3072)

    def wst_load(hp):
        s = hp % 2
        if hp >= 2 and ("projdone", hp - 2) in mk:
            LS.append(lambda e, m=mk[("projdone", hp - 2)]: e.wait_ge(SM["cP"], m))
        for grp, rbase in ((0, 0), (1, 1024), (2, 2048)):
            for dd in range(8):
                src = t["w_in_bf"][rbase + hp * 128: rbase + (hp + 1) * 128,
                                   dd * 128:(dd + 1) * 128]
                dst = WST(s)[:, (grp * 8 + dd) * 128:(grp * 8 + dd + 1) * 128]
                LS.append(lambda e, ss=src, dd_=dst: e.dma_start_transpose(dd_, ss).then_inc(SM["sW"], 16))
                b.inc("sW", 16)
        mk[("wst", hp)] = C["sW"]

    def wslab_gather(hp, qt):
        s = qt % 2
        key = ("sim", hp, qt - 2, 31) if qt >= 2 else ("sim", hp - 1, qt + 2, 31)
        if key in mk:
            LG.append(lambda e, m=mk[key]: e.wait_ge(SM["cP"], m))
        for sl, (head_off, nf) in enumerate([(0, 0), (0, 1), (1, 0), (1, 1)]):
            col = (2 * hp + head_off) * 8 + qt * 2 + nf
            dst = b.wb(WSL(s) + (head_off * 2 + nf) * WSLAB, WSLAB)
            src = bass.AP(t["ebvec"].tensor, 0, [[HOUT * VECW, 1], [1, HOUT * VECW]])
            idxap = t["idxs"][:, col:col + 1]
            LG.append(lambda e, d=dst, s_=src, ix=idxap:
                      e.indirect_dma_start(d, None, s_, bass.IndirectOffsetOnAxis(ix, 1)).then_inc(SM["sG"], 16))
            b.inc("sG", 16)
        mk[("wslab", hp, qt)] = C["sG"]

    def proj_groups(hp):
        gs = []
        s = hp % 2
        for grp, nt, dsts in ((0, 8, "K"), (2, 4, "Q"), (1, 8, "V")):
            for tb in range(nt):
                gs.append((grp, tb))
        return gs

    def emit_proj_group(hp, gidx, grp, tb):
        s = hp % 2
        LP.append(lambda e, m=mk[("wst", hp)]: e.wait_ge(SM["sW"], m))
        if gidx > 0:
            LP.append(lambda e, m=mk[("pevac", hp, gidx - 1)]: e.wait_ge(SM["cO"], m))
        if grp == 2:
            rhs_cols = (tb * 512, 512)
        else:
            rhs_cols = (tb * 512, 512)
        for dd in range(8):
            lhsT = WST(s)[:, (grp * 8 + dd) * 128:(grp * 8 + dd + 1) * 128]
            rhs = b.xbf(dd)[:, rhs_cols[0]:rhs_cols[0] + rhs_cols[1]]
            is_last = dd == 7
            LP.append(lambda e, l=lhsT, r=rhs, st=(dd == 0), sp=is_last:
                      e.matmul(t["pb2"][:], l, r, start=st, stop=sp))
        b.pinc(LP)
        b.inc("cP")
        mk[("pgrp", hp, gidx)] = C["cP"]
        LV.append(lambda e, m=mk[("pgrp", hp, gidx)]: e.wait_ge(SM["cP"], m))
        if grp == 0:
            dst = KS(s)[:, tb * 512:(tb + 1) * 512]
            LV.append(lambda e, d=dst: e.tensor_copy(out=d, in_=t["pb2"][:]).then_inc(SM["cO"], 1))
            b.inc("cO")
            mk[("pevac", hp, gidx)] = C["cO"]
        elif grp == 2:
            dst = QS(s)[:, tb * 512:(tb + 1) * 512]
            LV.append(lambda e, d=dst: e.tensor_scalar(out=d, in0=t["pb2"][:],
                                                       scalar1=0.125, scalar2=None,
                                                       op0=ALU.mult).then_inc(SM["cO"], 1))
            b.inc("cO")
            mk[("pevac", hp, gidx)] = C["cO"]
        else:
            LV.append(lambda e: e.tensor_copy(out=b.wb(A_VST, 512), in_=t["pb2"][:]).then_inc(SM["cO"], 1))
            b.inc("cO")
            mk[("vst", hp, tb)] = C["cO"]
            LP.append(lambda e, m=mk[("vst", hp, tb)]: e.wait_ge(SM["cO"], m))
            if tb > 0:
                LP.append(lambda e, m=mk[("vpev", hp, tb - 1)]: e.wait_ge(SM["cO"], m))
            for cc in range(4):
                LP.append(lambda e, c4=cc: e.transpose(t["pt"][:, c4 * 128:(c4 + 1) * 128],
                                                       b.wb(A_VST + c4 * 128, 128),
                                                       t["cstb"][:, CB_ID:CB_ID + 128]))
            b.pinc(LP)
            b.inc("cP")
            mk[("vtp", hp, tb)] = C["cP"]
            LV.append(lambda e, m=mk[("vtp", hp, tb)]: e.wait_ge(SM["cP"], m))
            for cc in range(4):
                kc = tb * 4 + cc
                off = VPo(s) + kc * 130
                dst = bass.AP(t["WS"], off, [[55040, 128], [65, 2], [1, 64]])
                srcap = bass.AP(t["pt"], cc * 128, [[1024, 128], [64, 2], [1, 64]])
                LV.append(lambda e, d=dst, s_=srcap: e.tensor_copy(out=d, in_=s_))
            b.dnop(LV, "cO")
            b.inc("cO")
            mk[("vpev", hp, tb)] = C["cO"]
            mk[("pevac", hp, gidx)] = C["cO"]
        if gidx == 19:
            mk[("projdone", hp)] = mk[("pgrp", hp, gidx)]

    def emit_pv(hp, qt, kcv):
        s = hp % 2
        parv = kcv % 2
        LP.append(lambda e, m=mk[("exp", hp, qt, kcv)]: e.wait_ge(SM["cA"], m))
        if kcv == 0:
            key = ("ust", hp, qt - 1) if qt >= 1 else (("ust", hp - 1, 3) if hp >= 1 else None)
            if key and key in mk:
                LP.append(lambda e, m=mk[key]: e.wait_ge(SM["cV"], m))
        st = kcv == 0
        sp = kcv == 31
        LP.append(lambda e, k=kcv, p=parv, st_=st, sp_=sp:
                  e.matmul(t["pb0"][0:65, :], b.wb(VPo(s) + k * 130, 65),
                           b.wb(A_PT + p * 1024, 512), start=st_, stop=sp_))
        LP.append(lambda e, k=kcv, p=parv, st_=st, sp_=sp:
                  e.matmul(t["pb1"][0:65, :], b.wb(VPo(s) + k * 130 + 65, 65),
                           b.wb(A_PT + p * 1024 + 512, 512),
                           start=st_, stop=sp_).then_inc(SM["cP"], 1))
        b.inc("cP")
        mk[("pv", hp, qt, kcv)] = C["cP"]

    wst_load(0)
    wslab_gather(0, 0)
    for s in range(2):
        LV.append(lambda e, s_=s: e.memset(b.wb(VPo(s_), 4160), 1.0))
    b.dnop(LV, "cO")
    b.inc("cO")
    for gi, (grp, tb) in enumerate(proj_groups(0)):
        emit_proj_group(0, gi, grp, tb)
    mk[("pevacall", 0)] = C["cO"]
    wslab_gather(0, 1)

    for hp in range(8):
        s = hp % 2
        if hp < 7:
            wst_load(hp + 1)
        pg = proj_groups(hp + 1) if hp < 7 else []
        pgi = 0
        for qt in range(4):
            if qt == 0 and hp >= 1:
                wslab_gather(hp, 1)
            if qt >= 1 and qt + 1 <= 3:
                wslab_gather(hp, qt + 1)
            if qt == 3 and hp < 7:
                wslab_gather(hp + 1, 0)
            for kc in range(32):
                par = kc % 2
                if kc >= 2:
                    LP.append(lambda e, m=mk[("exp", hp, qt, kc - 2)]: e.wait_ge(SM["cA"], m))
                elif qt >= 1:
                    LP.append(lambda e, m=mk[("exp", hp, qt - 1, 30 + kc)]: e.wait_ge(SM["cA"], m))
                elif hp >= 1:
                    LP.append(lambda e, m=mk[("exp", hp - 1, 3, 30 + kc)]: e.wait_ge(SM["cA"], m))
                if kc == 1:
                    rzkey = ("rzbe", hp, qt - 1) if qt >= 1 else ("rzbe", hp - 1, 3)
                    if rzkey in mk:
                        LP.append(lambda e, m=mk[rzkey]: e.wait_ge(SM["cA"], m))
                if kc == 0 and qt == 0:
                    LP.append(lambda e, m=mk[("pevacall", hp)]: e.wait_ge(SM["cO"], m))
                if kc == 0:
                    LP.append(lambda e, m=mk[("wslab", hp, qt)]: e.wait_ge(SM["sG"], m))
                LP.append(lambda e, k=kc, q=qt, p=par, s_=s:
                          e.matmul(t["pw"][:, p * 1024:p * 1024 + 512],
                                   KS(s_)[0:64, k * 128:(k + 1) * 128],
                                   QS(s_)[0:64, q * 512:(q + 1) * 512],
                                   start=True, stop=False, tile_position=(0, 0)))
                LP.append(lambda e, k=kc, q=qt, p=par, s_=s:
                          e.matmul(t["pw"][:, p * 1024 + 512:p * 1024 + 1024],
                                   KS(s_)[64:128, k * 128:(k + 1) * 128],
                                   QS(s_)[64:128, q * 512:(q + 1) * 512],
                                   start=True, stop=False,
                                   tile_position=(64, 0)))
                nf = 0 if kc < 16 else 1
                voff = 1920 - (kc % 16) * 128
                slab = WSL(qt % 2)
                for ho in range(2):
                    ap = b.wb(slab + (ho * 2 + nf) * WSLAB + voff, 512)
                    m = LP.append(lambda e, p=par, h2=ho, a_=ap: e.matmul(
                        t["pw"][:, p * 1024 + h2 * 512:p * 1024 + h2 * 512 + 512],
                        t["cstb"][:, CB_ID:CB_ID + 128], a_,
                        start=False, stop=True))
                b.pinc(LP)
                b.inc("cP")
                mk[("sim", hp, qt, kc)] = C["cP"]

                LA.append(lambda e, m=mk[("sim", hp, qt, kc)]: e.wait_ge(SM["cP"], m))
                pvkey = None
                if kc >= 2:
                    pvkey = ("pv", hp, qt, kc - 2)
                elif qt >= 1:
                    pvkey = ("pv", hp, qt - 1, 30 + kc)
                elif hp >= 1:
                    pvkey = ("pv", hp - 1, 3, 30 + kc)
                if pvkey is not None and pvkey in mk:
                    LA.append(lambda e, m=mk[pvkey]: e.wait_ge(SM["cP"], m))
                LA.append(lambda e, p=par:
                          e.activation(out=b.wb(A_PT + p * 1024, 1024),
                                       in_=t["pw"][:, p * 1024:p * 1024 + 1024],
                                       func=AF.Exp).then_inc(SM["cA"], 1))
                b.inc("cA")
                mk[("exp", hp, qt, kc)] = C["cA"]

                if kc >= 1:
                    emit_pv(hp, qt, kc - 1)
                if kc % 4 == 3 and pgi < len(pg):
                    emit_proj_group(hp + 1, pgi, *pg[pgi])
                    pgi += 1
            emit_pv(hp, qt, 31)
            LV.append(lambda e, m=mk[("pv", hp, qt, 31)]: e.wait_ge(SM["cP"], m))
            LV.append(lambda e: e.reciprocal(out=b.wf(A_RZF, 512)[64:65, :],
                                             in_=t["pb0"][64:65, :]))
            LV.append(lambda e: e.reciprocal(out=b.wf(A_RZF + 512, 512)[64:65, :],
                                             in_=t["pb1"][64:65, :]).then_inc(SM["cV"], 1))
            b.inc("cV")
            mk[("rz", hp, qt)] = C["cV"]
            LP.append(lambda e, m=mk[("rz", hp, qt)]: e.wait_ge(SM["cV"], m))
            LP.append(lambda e, m=mk[("exp", hp, qt, 31)]: e.wait_ge(SM["cA"], m))
            LV.append(lambda e: e.drain())
            LV.append(lambda e: e.tensor_copy(out=b.wb(A_RZB, 512)[64:65, :],
                                              in_=b.wf(A_RZF, 512)[64:65, :]))
            LV.append(lambda e: e.tensor_copy(out=b.wb(A_RZB + 512, 512)[64:65, :],
                                              in_=b.wf(A_RZF + 512, 512)[64:65, :]).then_inc(SM["cV"], 1))
            b.inc("cV")
            mk[("rzc", hp, qt)] = C["cV"]
            LP.append(lambda e, m=mk[("rzc", hp, qt)]: e.wait_ge(SM["cV"], m))
            LP.append(lambda e: e.matmul(t["pw"][0:64, 1024:1536],
                                         t["cstb"][64:65, CB_ONES:CB_ONES + 64],
                                         b.wb(A_RZB, 512)[64:65, :],
                                         start=True, stop=True, tile_position=(64, 0)))
            LP.append(lambda e: e.matmul(t["pw"][0:64, 1536:2048],
                                         t["cstb"][64:65, CB_ONES:CB_ONES + 64],
                                         b.wb(A_RZB + 512, 512)[64:65, :],
                                         start=True, stop=True,
                                         tile_position=(64, 0)).then_inc(SM["cP"], 1))
            b.inc("cP")
            mk[("rzbm", hp, qt)] = C["cP"]
            LA.append(lambda e, m=mk[("rzbm", hp, qt)]: e.wait_ge(SM["cP"], m))
            LA.append(lambda e: e.activation(out=b.wb(A_RZB, 512)[0:64, :],
                                             in_=t["pw"][0:64, 1024:1536], func=AF.Copy))
            LA.append(lambda e: e.activation(out=b.wb(A_RZB + 512, 512)[0:64, :],
                                             in_=t["pw"][0:64, 1536:2048],
                                             func=AF.Copy).then_inc(SM["cA"], 1))
            b.inc("cA")
            mk[("rzbe", hp, qt)] = C["cA"]
            us = A_UST + (qt % 2) * 1024
            LV.append(lambda e, m=mk[("rzbe", hp, qt)]: e.wait_ge(SM["cA"], m))
            if ("ustore", hp, qt - 2) in mk:
                LV.append(lambda e, m=mk[("ustore", hp, qt - 2)]: e.wait_ge(SM["sG"], m))
            elif ("ustore", hp - 1, qt + 2) in mk:
                LV.append(lambda e, m=mk[("ustore", hp - 1, qt + 2)]: e.wait_ge(SM["sG"], m))
            LV.append(lambda e, u=us: e.tensor_tensor(out=b.wb(u, 512)[0:64, :],
                                                      in0=t["pb0"][0:64, :],
                                                      in1=b.wb(A_RZB, 512)[0:64, :], op=ALU.mult))
            LV.append(lambda e, u=us: e.tensor_tensor(out=b.wb(u + 512, 512)[0:64, :],
                                                      in0=t["pb1"][0:64, :],
                                                      in1=b.wb(A_RZB + 512, 512)[0:64, :],
                                                      op=ALU.mult).then_inc(SM["cV"], 1))
            b.inc("cV")
            mk[("ust", hp, qt)] = C["cV"]
            LG.append(lambda e, m=mk[("ust", hp, qt)]: e.wait_ge(SM["cV"], m))
            LG.append(lambda e, u=us, q=qt, h=hp: e.dma_start(
                out=t["u_sc"][h, 0:64, q * 512:(q + 1) * 512],
                in_=b.wb(u, 512)[0:64, :]).then_inc(SM["sG"], 16))
            LG.append(lambda e, u=us, q=qt, h=hp: e.dma_start(
                out=t["u_sc"][h, 64:128, q * 512:(q + 1) * 512],
                in_=b.wb(u + 512, 512)[0:64, :]).then_inc(SM["sG"], 16))
            b.inc("sG", 32)
            mk[("ustore", hp, qt)] = C["sG"]
        while pgi < len(pg):
            emit_proj_group(hp + 1, pgi, *pg[pgi])
            pgi += 1
        if hp < 7:
            mk[("pevacall", hp + 1)] = C["cO"]

    if "ksb" in b.dbg:
        LG.append(lambda e, m=mk[("pevacall", 1)] if ("pevacall", 1) in mk else 0:
                  e.wait_ge(SM["cO"], m))
        LG.append(lambda e: e.dma_start(out=dbg_out["ksb"], in_=KS(1)).then_inc(SM["sG"], 16))
        b.inc("sG", 16)
    if "qsb" in b.dbg:
        LG.append(lambda e: e.dma_start(out=dbg_out["qsb"], in_=QS(1)).then_inc(SM["sG"], 16))
        b.inc("sG", 16)
    if "vpr" in b.dbg:
        LG.append(lambda e: e.dma_start(out=dbg_out["vpr"], in_=b.wb(VPo(1), 4160)).then_inc(SM["sG"], 16))
        b.inc("sG", 16)
    if "u8" in b.dbg:
        LG.append(lambda e, m=mk[("ustore", 7, 3)]: e.wait_ge(SM["sG"], m))
        for c in range(8):
            LG.append(lambda e, cc=c: e.dma_start(out=dbg_out["u8"][cc],
                                                  in_=t["u_sc"][cc]).then_inc(SM["sG"], 16))
        b.inc("sG", 8 * 16)

    block.tensor(lambda e: [f(e) for f in LP])
    block.scalar(lambda e: [f(e) for f in LA])
    block.vector(lambda e: [f(e) for f in LV])
    block.gpsimd(lambda e: [f(e) for f in LG])
    block.sync(lambda e: [f(e) for f in LS])


def emit_poll(LG, b, epoch, semname):
    SM, t = b.SM, b.t

    def op(e, ep=epoch, sn=semname):
        with (
            e.register("done" + sn) as done,
            e.register("cnt" + sn) as cnt,
            e.register("iters" + sn) as it,
            e.register("fv" + sn) as fv,
            e.register("bound" + sn) as bo,
        ):
            e.reg_mov(done, 1)
            e.reg_mov(cnt, 0)
            e.reg_mov(it, 0)
            with e.While(done):
                e.reg_add(cnt, cnt, 16)
                e.reg_add(it, it, 1)
                src = bass.AP(t["shFlag"].tensor, 0, [[32, 1], [1, 32]])
                e.indirect_dma_start(
                    t["flgin"][:], None, src,
                    bass.IndirectOffsetOnAxis(t["idxs"][0:2, 147:148], 1)).then_inc(SM[sn], 16)
                e.wait_ge(SM[sn], cnt)
                e.reg_load(fv, t["flgin"][0:1, 0:1])
                e.reg_alu(done, fv, ep, ALU.is_lt)
                e.reg_alu(bo, it, 500000, ALU.is_lt)
                e.reg_alu(done, done, bo, ALU.logical_and)
    LG.append(op)


def emit_flag_write(LG, b, epoch, wait_sg_to):
    SM, t = b.SM, b.t
    LG.append(lambda e, m=wait_sg_to: e.wait_ge(SM["sG"], m))

    def op(e, ep=epoch):
        dst = bass.AP(t["shFlag"].tensor, 0, [[32, 1], [1, 32]])
        e.indirect_dma_start(
            dst, bass.IndirectOffsetOnAxis(t["idxs"][0:2, 146:147], 1),
            t["flg"][32 * (ep - 1):32 * (ep - 1) + 2, :], None).then_inc(SM["sG"], 16)
    LG.append(op)
    b.inc("sG", 16)


E_R0_F = 8192
E_R1_F = E_R0_F + 2048
E_R2_F = E_R1_F + 2048
E_R3_F = E_R2_F + 2048
E_ABB_F = E_R3_F + 2048
E_BBB_F = E_ABB_F + 2048
E_XT2_F = E_BBB_F + 2048
E_CS2_F = E_XT2_F + 2048
E_RB_B = 2 * (E_CS2_F + 2048)


def emit_rowmath_ab(b, LP, LA, LV, LG, n, div, tag,
                    R0F=None, R1F=None, R2F=None, R3F=None,
                    ABBF=None, BBBF=None, RBB=None):
    SM, t, mk, C = b.SM, b.t, b.mk, b.C
    R0F = E_R0_F if R0F is None else R0F
    R1F = E_R1_F if R1F is None else R1F
    R2F = E_R2_F if R2F is None else R2F
    R3F = E_R3_F if R3F is None else R3F
    ABBF = E_ABB_F if ABBF is None else ABBF
    BBBF = E_BBB_F if BBBF is None else BBBF
    RBB = E_RB_B if RBB is None else RBB
    LA.append(lambda e, m=mk[(tag, "smm", 7)]: e.wait_ge(SM["cP"], m))
    LA.append(lambda e: e.activation(out=b.wf(R0F, n)[0:1, :],
                                     in_=t["pw"][0:1, 0:n], func=AF.Copy))
    LA.append(lambda e: e.activation(out=b.wf(R1F, n)[32:33, :],
                                     in_=t["pw"][32:33, 0:n],
                                     func=AF.Copy).then_inc(SM["cA"], 1))
    b.inc("cA")
    mk[(tag, "sev")] = C["cA"]
    LG.append(lambda e, m=mk[(tag, "sev")]: e.wait_ge(SM["cA"], m))
    LG.append(lambda e: e.dma_start(out=b.wf(R1F, n)[0:1, :],
                                    in_=b.wf(R1F, n)[32:33, :]).then_inc(SM["sZ"], 16))
    b.inc("sZ", 16)
    mk[(tag, "s2mv")] = C["sZ"]
    R0 = b.wf(R0F, n)[0:1, :]
    R1 = b.wf(R1F, n)[0:1, :]
    R2 = b.wf(R2F, n)[0:1, :]
    R3 = b.wf(R3F, n)[0:1, :]
    LV.append(lambda e, m=mk[(tag, "s2mv")]: e.wait_ge(SM["sZ"], m))
    LV.append(lambda e: e.tensor_scalar(out=R0, in0=R0, scalar1=1.0 / div,
                                        scalar2=None, op0=ALU.mult))
    LV.append(lambda e: e.tensor_scalar(out=R1, in0=R1, scalar1=1.0 / div,
                                        scalar2=None, op0=ALU.mult))
    LV.append(lambda e: e.drain())
    LV.append(lambda e: e.tensor_mul(out=R2, in0=R0, in1=R0))
    LV.append(lambda e: e.drain())
    LV.append(lambda e: e.tensor_sub(out=R1, in0=R1, in1=R2).then_inc(SM["cV"], 1))
    b.inc("cV")
    mk[(tag, "var")] = C["cV"]
    LA.append(lambda e, m=mk[(tag, "var")]: e.wait_ge(SM["cV"], m))
    LA.append(lambda e: e.activation(out=R2, in_=R1, func=AF.Ln,
                                     bias=t["row"][0:1, 10:11]))
    LA.append(lambda e: e.drain())
    LA.append(lambda e: e.activation(out=R2, in_=R2, func=AF.Exp,
                                     scale=-0.5).then_inc(SM["cA"], 1))
    b.inc("cA")
    mk[(tag, "r")] = C["cA"]
    LV.append(lambda e, m=mk[(tag, "r")]: e.wait_ge(SM["cA"], m))
    LV.append(lambda e: e.tensor_mul(out=R3, in0=R0, in1=R2))
    LV.append(lambda e: e.drain())
    LV.append(lambda e: e.tensor_copy(out=b.wb(RBB, n)[0:1, :], in_=R2))
    LV.append(lambda e: e.tensor_copy(out=b.wb(RBB + 2048, n)[0:1, :],
                                      in_=R3).then_inc(SM["cV"], 1))
    b.inc("cV")
    mk[(tag, "rows")] = C["cV"]
    nt = (n + 511) // 512
    LP.append(lambda e, m=mk[(tag, "rows")]: e.wait_ge(SM["cV"], m))
    for tb in range(nt):
        w = min(512, n - tb * 512)
        LP.append(lambda e, tb_=tb, w_=w: e.matmul(
            t["pw"][:, tb_ * 512:tb_ * 512 + w_],
            t["cstb"][0:1, CB_ONES:CB_ONES + 128],
            b.wb(RBB + tb_ * 512, w_)[0:1, :], start=True, stop=True))
    b.pinc(LP)
    b.inc("cP")
    mk[(tag, "abm")] = C["cP"]
    LA.append(lambda e, m=mk[(tag, "abm")]: e.wait_ge(SM["cP"], m))
    LA.append(lambda e, n_=n: e.activation(out=b.wf(ABBF, n_), in_=t["pw"][:, 0:n_],
                                           func=AF.Copy).then_inc(SM["cA"], 1))
    b.inc("cA")
    mk[(tag, "abe")] = C["cA"]
    LP.append(lambda e, m=mk[(tag, "abe")]: e.wait_ge(SM["cA"], m))
    for tb in range(nt):
        w = min(512, n - tb * 512)
        LP.append(lambda e, tb_=tb, w_=w: e.matmul(
            t["pw"][:, tb_ * 512:tb_ * 512 + w_],
            t["cstb"][0:1, CB_ONES:CB_ONES + 128],
            b.wb(RBB + 2048 + tb_ * 512, w_)[0:1, :], start=True, stop=True))
    b.pinc(LP)
    b.inc("cP")
    mk[(tag, "bbm")] = C["cP"]
    LA.append(lambda e, m=mk[(tag, "bbm")]: e.wait_ge(SM["cP"], m))
    LA.append(lambda e, n_=n: e.activation(out=b.wf(BBBF, n_), in_=t["pw"][:, 0:n_],
                                           func=AF.Copy, scale=-1.0).then_inc(SM["cA"], 1))
    b.inc("cA")
    mk[(tag, "ab")] = C["cA"]


def emit_stats_mm(b, LP, tag, c, nchunks, src_bf, src_sq, tb_n, wait_key=None):
    SM, t, mk, C = b.SM, b.t, b.mk, b.C
    if wait_key is not None:
        LP.append(lambda e, m=mk[wait_key]: e.wait_ge(SM["cA"], m))
    for tb in range(tb_n):
        LP.append(lambda e, tb_=tb, s=src_bf, cc=c: e.matmul(
            t["pw"][0:1, tb_ * 512:(tb_ + 1) * 512],
            t["cstb"][:, CB_ONES:CB_ONES + 1],
            s[:, tb_ * 512:(tb_ + 1) * 512],
            start=(cc == 0), stop=(cc == nchunks - 1), tile_position=(0, 0)))
        LP.append(lambda e, tb_=tb, s=src_sq, cc=c: e.matmul(
            t["pw"][32:33, tb_ * 512:(tb_ + 1) * 512],
            t["cstb"][:, CB_ONES:CB_ONES + 1],
            s[:, tb_ * 512:(tb_ + 1) * 512],
            start=(cc == 0), stop=(cc == nchunks - 1), tile_position=(0, 32)))
    b.pinc(LP)
    b.inc("cP")
    mk[(tag, "smm", c)] = C["cP"]


def ph_epilogue1(b, dbg_out):
    block, SM, t, C, mk = b.block, b.SM, b.t, b.C, b.mk
    LP, LA, LV, LG, LS = [], [], [], [], []

    for i in range(16):
        if i >= 2:
            LS.append(lambda e, m=mk[("rtp", i - 2)]: e.wait_ge(SM["cP"], m))
        LS.append(lambda e, ii=i: e.dma_start(
            out=b.wf(E_XT2_F + (ii % 2) * 1024, 1024),
            in_=t["x_own"][ii * 128:(ii + 1) * 128, :]).then_inc(SM["sX"], 16))
        b.inc("sX", 16)
        mkld = C["sX"]
        LP.append(lambda e, m=mkld: e.wait_ge(SM["sX"], m))
        if i >= 1:
            LP.append(lambda e, m=mk[("rev", i - 1)]: e.wait_ge(SM["cV"], m))
        for c in range(8):
            LP.append(lambda e, ii=i, cc=c: e.transpose(
                t["pw"][:, cc * 128:(cc + 1) * 128],
                b.wf(E_XT2_F + (ii % 2) * 1024 + cc * 128, 128),
                t["cst"][:, CS_IDF:CS_IDF + 128]))
        b.pinc(LP)
        b.inc("cP")
        mk[("rtp", i)] = C["cP"]
        LV.append(lambda e, m=mk[("rtp", i)]: e.wait_ge(SM["cP"], m))
        for c in range(8):
            LV.append(lambda e, ii=i, cc=c: e.tensor_copy(
                out=b.Rown(cc)[:, ii * 128:(ii + 1) * 128],
                in_=t["pw"][:, cc * 128:(cc + 1) * 128]))
        b.dnop(LV, "cV")
        b.inc("cV")
        mk[("rev", i)] = C["cV"]
    mk["rfill"] = C["cV"]
    LV.append(lambda e: [e.memset(b.Rcol(c, 0, 2), 0.0) for c in range(8)])
    b.dnop(LV, "cV")
    b.inc("cV")

    for c in range(8):
        if c >= 2:
            LG.append(lambda e, m=mk[("ep1", "smm", c - 2)]: e.wait_ge(SM["cP"], m))
        LG.append(lambda e, cc=c: e.dma_start(
            out=b.wb((cc % 2) * 2048, 2048),
            in_=t["u_sc"][cc]).then_inc(SM["sG"], 16))
        b.inc("sG", 16)
        mk[("uld", c)] = C["sG"]
        LA.append(lambda e, m=mk[("uld", c)]: e.wait_ge(SM["sG"], m))
        if c >= 1:
            LA.append(lambda e, m=mk[("ep1", "smm", c - 1)]: e.wait_ge(SM["cP"], m))
        LA.append(lambda e, cc=c: e.activation(out=b.wb(2 * 2048, 2048),
                                               in_=b.wb((cc % 2) * 2048, 2048),
                                               func=AF.Square).then_inc(SM["cA"], 1))
        b.inc("cA")
        mk[("usq", c)] = C["cA"]
        if c == 0:
            LP.append(lambda e, m=mk["rfill"]: e.wait_ge(SM["cV"], m))
        emit_stats_mm(b, LP, "ep1", c, 8, b.wb((c % 2) * 2048, 2048),
                      b.wb(2 * 2048, 2048), 4, wait_key=("usq", c))
    emit_rowmath_ab(b, LP, LA, LV, LG, 2048, D, "ep1")

    for c in range(8):
        LG.append(lambda e, m=mk[("ep1", "ab")]: e.wait_ge(SM["cA"], m))
        if c >= 2:
            LG.append(lambda e, m=mk[("nrm", c - 2)]: e.wait_ge(SM["cV"], m))
        LG.append(lambda e, cc=c: e.dma_start(
            out=b.wb((cc % 2) * 2048, 2048),
            in_=t["u_sc"][cc]).then_inc(SM["sG"], 16))
        b.inc("sG", 16)
        mk[("uld2", c)] = C["sG"]
        LV.append(lambda e, m=mk[("uld2", c)]: e.wait_ge(SM["sG"], m))
        if c == 0:
            LV.append(lambda e, m=mk[("ep1", "ab")]: e.wait_ge(SM["cA"], m))
        LV.append(lambda e, cc=c: e.tensor_tensor(
            out=b.wf(E_CS2_F, 2048), in0=b.wb((cc % 2) * 2048, 2048),
            in1=b.wf(E_ABB_F, 2048), op=ALU.mult))
        LV.append(lambda e: e.drain())
        LV.append(lambda e: e.tensor_tensor(
            out=b.wf(E_CS2_F, 2048), in0=b.wf(E_CS2_F, 2048),
            in1=b.wf(E_BBB_F, 2048), op=ALU.add))
        LV.append(lambda e: e.drain())
        LV.append(lambda e, cc=c: e.tensor_tensor(
            out=b.Rown(cc), in0=b.Rown(cc),
            in1=b.wf(E_CS2_F, 2048), op=ALU.add).then_inc(SM["cV"], 1))
        b.inc("cV")
        mk[("nrm", c)] = C["cV"]
    mk["xsand"] = C["cV"]

    for c in range(8):
        LG.append(lambda e, m=mk[("nrm", c)]: e.wait_ge(SM["cV"], m))
        src = b.Rcol(c, 2 + TOWN - HALO, HALO)
        dstap = bass.AP(t["shE1"].tensor, 0, [[HALO, 2 * NCH * 128], [1, HALO]])
        LG.append(lambda e, s_=src, d=dstap, cc=c: e.indirect_dma_start(
            d, bass.IndirectOffsetOnAxis(t["idxs"][:, 128 + cc:129 + cc], 0),
            s_, None).then_inc(SM["sG"], 16))
        b.inc("sG", 16)
    emit_flag_write(LG, b, 1, C["sG"])

    if "xsand" in b.dbg:
        LG.append(lambda e, m=mk["xsand"]: e.wait_ge(SM["cV"], m))
        for c in range(8):
            LG.append(lambda e, cc=c: e.dma_start(out=dbg_out["xsand"][cc],
                                                  in_=b.Rown(cc)).then_inc(SM["sG"], 16))
        b.inc("sG", 8 * 16)

    block.tensor(lambda e: [f(e) for f in LP])
    block.scalar(lambda e: [f(e) for f in LA])
    block.vector(lambda e: [f(e) for f in LV])
    block.gpsimd(lambda e: [f(e) for f in LG])
    block.sync(lambda e: [f(e) for f in LS])


def ph_pre_ffn(b):
    block, SM, t, C, mk = b.block, b.SM, b.t, b.C, b.mk
    LP, LA, LV, LG, LS = [], [], [], [], []

    for c in range(8):
        LG.append(lambda e, cc=c: e.dma_start(out=t["xs_sc"][cc],
                                              in_=b.Rown(cc)).then_inc(SM["sG"], 16))
    b.inc("sG", 8 * 16)
    mk["xssave"] = C["sG"]

    emit_poll(LG, b, 1, "sZ1")
    for c in range(8):
        srcap = bass.AP(t["shE1"].tensor, 0, [[HALO, 2 * NCH * 128], [1, HALO]])
        LG.append(lambda e, s_=srcap, cc=c: e.indirect_dma_start(
            b.rhc(cc)[:, 2:516], None, s_,
            bass.IndirectOffsetOnAxis(t["idxs"][:, 136 + cc:137 + cc], 0)).then_inc(SM["sG"], 16))
    b.inc("sG", 8 * 16)
    mk["e1g"] = C["sG"]
    LV.append(lambda e, m=mk["e1g"]: e.wait_ge(SM["sG"], m))
    LV.append(lambda e: e.tensor_scalar(out=t["rh"][:], in0=t["rh"][:],
                                        scalar1=t["cst"][:, CS_KEEP:CS_KEEP + 1],
                                        scalar2=None, op0=ALU.mult).then_inc(SM["cV"], 1))
    b.inc("cV")
    mk["e1k"] = C["cV"]

    for c in range(8):
        LA.append(lambda e, m=mk["e1k"]: e.wait_ge(SM["cV"], m))
        if c >= 1:
            LA.append(lambda e, m=mk[("pf_h", "smm", c - 1)]: e.wait_ge(SM["cP"], m))
        LA.append(lambda e, cc=c: e.activation(out=b.wb(0, 514),
                                               in_=b.rhc(cc)[:, 2:516], func=AF.Copy))
        LA.append(lambda e, cc=c: e.activation(out=b.wb(2048, 514),
                                               in_=b.rhc(cc)[:, 2:516],
                                               func=AF.Square).then_inc(SM["cA"], 1))
        b.inc("cA")
        mk[("pf_hc", c)] = C["cA"]
        emit_stats_mm2(b, LP, "pf_h", c, 8, b.wb(0, 514), b.wb(2048, 514),
                       [512, 2], wait_key=("pf_hc", c))
    emit_rowmath_ab(b, LP, LA, LV, LG, 514, D, "pf_h")
    for c in range(8):
        LV.append(lambda e, m=mk[("pf_h", "ab")]: e.wait_ge(SM["cA"], m))
        LV.append(lambda e, cc=c: e.tensor_tensor(
            out=b.wf(E_CS2_F, 514), in0=b.rhc(cc)[:, 2:516],
            in1=b.wf(E_ABB_F, 514), op=ALU.mult))
        LV.append(lambda e: e.drain())
        LV.append(lambda e, cc=c: e.tensor_tensor(
            out=b.rhc(cc)[:, 2:516], in0=b.wf(E_CS2_F, 514),
            in1=b.wf(E_BBB_F, 514), op=ALU.add).then_inc(SM["cV"], 1))
        b.inc("cV")
    mk["y0h"] = C["cV"]

    for c in range(8):
        if c >= 1:
            LA.append(lambda e, m=mk[("pf_o", "smm", c - 1)]: e.wait_ge(SM["cP"], m))
        LA.append(lambda e, cc=c: e.activation(out=b.wb(0, 2048),
                                               in_=b.Rown(cc), func=AF.Copy))
        LA.append(lambda e, cc=c: e.activation(out=b.wb(2048, 2048),
                                               in_=b.Rown(cc),
                                               func=AF.Square).then_inc(SM["cA"], 1))
        b.inc("cA")
        mk[("pf_oc", c)] = C["cA"]
        emit_stats_mm2(b, LP, "pf_o", c, 8, b.wb(0, 2048), b.wb(2048, 2048),
                       [512] * 4, wait_key=("pf_oc", c))
    emit_rowmath_ab(b, LP, LA, LV, LG, 2048, D, "pf_o")
    for c in range(8):
        LV.append(lambda e, m=mk[("pf_o", "ab")]: e.wait_ge(SM["cA"], m))
        if c == 0:
            LV.append(lambda e, m=mk["xssave"]: e.wait_ge(SM["sG"], m))
        LV.append(lambda e, cc=c: e.tensor_tensor(
            out=b.wf(E_CS2_F, 2048), in0=b.Rown(cc),
            in1=b.wf(E_ABB_F, 2048), op=ALU.mult))
        LV.append(lambda e: e.drain())
        LV.append(lambda e, cc=c: e.tensor_tensor(
            out=b.Rown(cc), in0=b.wf(E_CS2_F, 2048),
            in1=b.wf(E_BBB_F, 2048), op=ALU.add).then_inc(SM["cV"], 1))
        b.inc("cV")
    mk["y0"] = C["cV"]

    block.tensor(lambda e: [f(e) for f in LP])
    block.scalar(lambda e: [f(e) for f in LA])
    block.vector(lambda e: [f(e) for f in LV])
    block.gpsimd(lambda e: [f(e) for f in LG])
    block.sync(lambda e: [f(e) for f in LS])


def emit_stats_mm2(b, LP, tag, c, nchunks, src_bf, src_sq, tbws, wait_key=None):
    SM, t, mk, C = b.SM, b.t, b.mk, b.C
    if wait_key is not None:
        LP.append(lambda e, m=mk[wait_key]: e.wait_ge(SM["cA"], m))
    off = 0
    for w in tbws:
        LP.append(lambda e, o=off, w_=w, s=src_bf, cc=c: e.matmul(
            t["pw"][0:1, o:o + w_],
            t["cstb"][:, CB_ONES:CB_ONES + 1],
            s[:, o:o + w_],
            start=(cc == 0), stop=(cc == nchunks - 1), tile_position=(0, 0)))
        LP.append(lambda e, o=off, w_=w, s=src_sq, cc=c: e.matmul(
            t["pw"][32:33, o:o + w_],
            t["cstb"][:, CB_ONES:CB_ONES + 1],
            s[:, o:o + w_],
            start=(cc == 0), stop=(cc == nchunks - 1), tile_position=(0, 32)))
        off += w
    b.pinc(LP)
    b.inc("cP")
    mk[(tag, "smm", c)] = C["cP"]


def ph_e2(b):
    block, SM, t, C, mk = b.block, b.SM, b.t, b.C, b.mk
    LG = []
    for c in range(8):
        src = b.Rcol(c, 2 + TOWN - HALO, HALO)
        dstap = bass.AP(t["shE2"].tensor, 0, [[HALO, 2 * NCH * 128], [1, HALO]])
        LG.append(lambda e, s_=src, d=dstap, cc=c: e.indirect_dma_start(
            d, bass.IndirectOffsetOnAxis(t["idxs"][:, 128 + cc:129 + cc], 0),
            s_, None).then_inc(SM["sG"], 16))
        b.inc("sG", 16)
    emit_flag_write(LG, b, 3, C["sG"])
    emit_poll(LG, b, 3, "sZ3")
    for c in range(8):
        srcap = bass.AP(t["shE2"].tensor, 0, [[HALO, 2 * NCH * 128], [1, HALO]])
        LG.append(lambda e, s_=srcap, cc=c: e.indirect_dma_start(
            b.rhc(cc)[:, 2:516], None, s_,
            bass.IndirectOffsetOnAxis(t["idxs"][:, 136 + cc:137 + cc], 0)).then_inc(SM["sG"], 16))
        b.inc("sG", 16)
    mk["e2g"] = C["sG"]
    LV = []
    LV.append(lambda e, m=mk["e2g"]: e.wait_ge(SM["sG"], m))
    LV.append(lambda e: e.tensor_scalar(out=t["rh"][:], in0=t["rh"][:],
                                        scalar1=t["cst"][:, CS_KEEP:CS_KEEP + 1],
                                        scalar2=None, op0=ALU.mult).then_inc(SM["cV"], 1))
    b.inc("cV")
    mk["e2k"] = C["cV"]
    b.block.gpsimd(lambda e: [f(e) for f in LG])
    b.block.vector(lambda e: [f(e) for f in LV])


def ph_final(b, y_out, sub=3):
    block, SM, t, C, mk = b.block, b.SM, b.t, b.C, b.mk
    LP, LA, LV, LG, LS = [], [], [], [], []
    for c in range(8):
        if c >= 1:
            LA.append(lambda e, m=mk[("fin", "smm", c - 1)]: e.wait_ge(SM["cP"], m))
        LA.append(lambda e, cc=c: e.activation(out=b.wb(0, 2048),
                                               in_=b.Rown(cc), func=AF.Copy))
        LA.append(lambda e, cc=c: e.activation(out=b.wb(2048, 2048),
                                               in_=b.Rown(cc),
                                               func=AF.Square).then_inc(SM["cA"], 1))
        b.inc("cA")
        mk[("fin_c", c)] = C["cA"]
        emit_stats_mm2(b, LP, "fin", c, 8, b.wb(0, 2048), b.wb(2048, 2048),
                       [512] * 4, wait_key=("fin_c", c))
    emit_rowmath_ab(b, LP, LA, LV, LG, 2048, D, "fin")
    if sub < 1:
        block.tensor(lambda e: [f(e) for f in LP])
        block.scalar(lambda e: [f(e) for f in LA])
        block.vector(lambda e: [f(e) for f in LV])
        block.gpsimd(lambda e: [f(e) for f in LG])
        block.sync(lambda e: [f(e) for f in LS])
        return
    for c in range(8):
        if c >= 2:
            LS.append(lambda e, m=mk[("fino", c - 2)]: e.wait_ge(SM["cV"], m))
        else:
            LS.append(lambda e, m=mk[("fin", "rows")]: e.wait_ge(SM["cV"], m))
        LS.append(lambda e, cc=c: e.dma_start(
            out=b.wf(E_R0_F + (cc % 2) * 2048, 2048),
            in_=t["xs_sc"][cc]).then_inc(SM["sX"], 16))
        b.inc("sX", 16)
        mk[("xsld", c)] = C["sX"]
        LV.append(lambda e, m=mk[("fin", "ab")]: e.wait_ge(SM["cA"], m))
        LV.append(lambda e, m=mk[("xsld", c)]: e.wait_ge(SM["sX"], m))
        LV.append(lambda e, cc=c: e.tensor_tensor(
            out=b.wf(E_CS2_F, 2048), in0=b.Rown(cc),
            in1=b.wf(E_ABB_F, 2048), op=ALU.mult))
        LV.append(lambda e: e.drain())
        LV.append(lambda e: e.tensor_tensor(
            out=b.wf(E_CS2_F, 2048), in0=b.wf(E_CS2_F, 2048),
            in1=b.wf(E_BBB_F, 2048), op=ALU.add))
        LV.append(lambda e: e.drain())
        LV.append(lambda e, cc=c: e.tensor_tensor(
            out=b.Rown(cc), in0=b.wf(E_R0_F + (cc % 2) * 2048, 2048),
            in1=b.wf(E_CS2_F, 2048), op=ALU.add).then_inc(SM["cV"], 1))
        b.inc("cV")
        mk[("fino", c)] = C["cV"]
    mk["findone"] = C["cV"]
    if sub < 2:
        block.tensor(lambda e: [f(e) for f in LP])
        block.scalar(lambda e: [f(e) for f in LA])
        block.vector(lambda e: [f(e) for f in LV])
        block.gpsimd(lambda e: [f(e) for f in LG])
        block.sync(lambda e: [f(e) for f in LS])
        return

    for i in range(16):
        LP.append(lambda e, m=mk["findone"]: e.wait_ge(SM["cV"], m))
        if i >= 2:
            LP.append(lambda e, m=mk[("fev", i - 2)]: e.wait_ge(SM["cA"], m))
        for c in range(8):
            LP.append(lambda e, ii=i, cc=c: e.transpose(
                t["pw"][:, (ii % 2) * 1024 + cc * 128:(ii % 2) * 1024 + (cc + 1) * 128],
                b.Rown(cc)[:, ii * 128:(ii + 1) * 128],
                t["cst"][:, CS_IDF:CS_IDF + 128]))
        b.pinc(LP)
        b.inc("cP")
        mk[("ftp", i)] = C["cP"]
        LA.append(lambda e, m=mk[("ftp", i)]: e.wait_ge(SM["cP"], m))
        if i >= 2:
            LA.append(lambda e, m=mk[("fst", i - 2)]: e.wait_ge(SM["sG"], m))
        LA.append(lambda e, ii=i: e.activation(
            out=b.wf(E_XT2_F + (ii % 2) * 1024, 1024),
            in_=t["pw"][:, (ii % 2) * 1024:(ii % 2) * 1024 + 1024],
            func=AF.Copy).then_inc(SM["cA"], 1))
        b.inc("cA")
        mk[("fev", i)] = C["cA"]
        LG.append(lambda e, m=mk[("fev", i)]: e.wait_ge(SM["cA"], m))
        LG.append(lambda e, ii=i: e.dma_start(
            out=y_out[ii * 128:(ii + 1) * 128, :],
            in_=b.wf(E_XT2_F + (ii % 2) * 1024, 1024)).then_inc(SM["sG"], 16))
        b.inc("sG", 16)
        mk[("fst", i)] = C["sG"]
    LG.append(lambda e, m=mk[("fst", 15)]: e.wait_ge(SM["sG"], m))

    block.tensor(lambda e: [f(e) for f in LP])
    block.scalar(lambda e: [f(e) for f in LA])
    block.vector(lambda e: [f(e) for f in LV])
    block.gpsimd(lambda e: [f(e) for f in LG])
    block.sync(lambda e: [f(e) for f in LS])


LY_B = 0
LW_VPR = 8224
LW_OSB = 18624
LW_AB = 20672
LW_KF = 22720
LW_QF = 32960
LW_WQS = 41152
LW_PST = 43200
LW_RZB = 45760
LW_QRW = 46272
LW_SQQ = 46784
LW_VST = 47296
LW_CX = 47808
LW_RN_F = 23936
LW_RNB = 49920
LW_MSK = 50432
LW_WOS_N = 52992
L2_CAST = 0
L2_R_F = 3072
L2_CS3_F = 9216
L2_ABB_F = 19456
L2_BBB_F = 21504
L2_RB_B = 47104
L2_XN = 22528
LH_HH = 0
LH_W1S = 11264
LH_W2S = 12800
LH_W2E_F = 7424


def ph_layer(b, l, dbg_out):
    block, SM, t, C, mk = b.block, b.SM, b.t, b.C, b.mk
    LP, LA, LV, LG, LS = [], [], [], [], []
    tg = "L%d" % l

    LS.append(lambda e: e.dma_start(out=b.wb(LW_MSK, 2560), in_=t["mw0"]).then_inc(SM["sW"], 16))
    b.inc("sW", 16)
    mk[(tg, "mw0")] = C["sW"]

    LV.append(lambda e: e.memset(b.wb(LW_VPR, 10400), 1.0))
    b.dnop(LV, "cV")
    b.inc("cV")
    mk[(tg, "vones")] = C["cV"]

    def ycast(tb):
        ypar = tb % 2
        lo = 3 + tb * 512
        if (tg, "ycfree", ypar) in mk:
            LA.append(lambda e, m=mk[(tg, "ycfree", ypar)]: e.wait_ge(SM["cP"], m))
        for c in range(8):
            dst0 = LY_B + ypar * 4112 + c * 514
            lo_h, hi = min(lo, 516), lo + 514
            hi_h = min(hi, 516)
            if hi_h > lo:
                n = hi_h - lo
                LA.append(lambda e, cc=c, d=dst0, a0=lo, n_=n:
                          e.activation(out=b.wb(d, n_), in_=b.rhc(cc)[:, a0:a0 + n_],
                                       func=AF.Copy))
            if hi > 516:
                a0 = max(lo, 516)
                n = hi - a0
                o = a0 - lo
                LA.append(lambda e, cc=c, d=dst0, a0_=a0 - 516, n_=n, o_=o:
                          e.activation(out=b.wb(d + o_, n_),
                                       in_=b.Rcol(cc, 2 + a0_, n_), func=AF.Copy))
        b.dnop(LA, "cA")
        b.inc("cA")
        mk[(tg, "yc", tb)] = C["cA"]

    def yrhs(tb, dd):
        ypar = tb % 2
        base = LY_B + ypar * 4112 + dd * 514
        off = 1 if dd < 4 else 0
        return b.wb(base + off, 512)

    def abprep(tb):
        if (tg, "abused", tb - 1) in mk:
            LS.append(lambda e, m=mk[(tg, "abused", tb - 1)]: e.wait_ge(SM["cV"], m))
        for j, src in enumerate(["rotcq", "rotsq", "rotck", "rotsk"]):
            LS.append(lambda e, s_=src, jj=j, tb_=tb: e.dma_start(
                out=b.wb(LW_AB + jj * 512, 512),
                in_=t[s_][:, tb_ * 512:(tb_ + 1) * 512]).then_inc(SM["sW"], 16))
        b.inc("sW", 4 * 16)
        mk[(tg, "abl", tb)] = C["sW"]
        LA.append(lambda e, m=mk[(tg, "abl", tb)]: e.wait_ge(SM["sW"], m))
        b.dnop(LA, "cA")
        b.inc("cA")
        mk[(tg, "abt", tb)] = C["cA"]

    prev = {"chain": None, "s2": None, "bank": {}}

    def wq_load(tb, e, par):
        if (tg, "wqfree", par) in mk:
            LS.append(lambda e_, m=mk[(tg, "wqfree", par)]: e_.wait_ge(SM["cP"], m))
        for dd in range(8):
            LS.append(lambda e_, dd_=dd, e2=e: e_.dma_start(
                out=b.wb(LW_WQS + par * 1024 + dd_ * 128, 128),
                in_=t["wqkv_bf"][l * D + dd_ * 128:l * D + (dd_ + 1) * 128,
                                 e2 * 128:(e2 + 1) * 128]).then_inc(SM["sW"], 16))
        b.inc("sW", 8 * 16)
        mk[(tg, "wql", tb, e)] = C["sW"]

    def qkv_mm(tb, e, par, bank):
        LP.append(lambda e_, m=mk[(tg, "wql", tb, e)]: e_.wait_ge(SM["sW"], m))
        LP.append(lambda e_, m=mk[(tg, "yc", tb)]: e_.wait_ge(SM["cA"], m))
        if bank in prev["bank"]:
            LP.append(lambda e_, m=prev["bank"][bank]: e_.wait_ge(SM["cA"], m))
        for dd in range(8):
            LP.append(lambda e_, dd_=dd, bk=bank, tb_=tb: e_.matmul(
                t[bk][:], b.wb(LW_WQS + par * 1024 + dd_ * 128, 128),
                yrhs(tb_, dd_), start=(dd_ == 0), stop=(dd_ == 7)))
        b.pinc(LP)
        b.inc("cP")
        mk[(tg, "wqfree", par)] = C["cP"]
        mk[(tg, "qkvm", tb, e)] = C["cP"]

    def qkprep(tb, e, bank, is_q):
        hp = e % 4
        LA.append(lambda e_, m=mk[(tg, "qkvm", tb, e)]: e_.wait_ge(SM["cP"], m))
        if prev["chain"] is not None:
            LA.append(lambda e_, m=prev["chain"]: e_.wait_ge(SM["cV"], m))
        sccol = (CS_QSC if is_q else CS_KSC) + l
        LA.append(lambda e_, bk=bank, sc_=sccol: e_.activation(
            out=b.wb(LW_QRW, 512), in_=t[bk][:], func=AF.Identity,
            scale=t["cst"][:, sc_:sc_ + 1]))
        LA.append(lambda e_, bk=bank: e_.activation(out=b.wb(LW_SQQ, 512), in_=t[bk][:],
                                                    func=AF.Square).then_inc(SM["cA"], 1))
        b.inc("cA")
        mk[(tg, "qkc", tb, e)] = C["cA"]
        prev["bank"][bank] = C["cA"]
        LP.append(lambda e_, m=mk[(tg, "qkc", tb, e)]: e_.wait_ge(SM["cA"], m))
        if prev["s2"] is not None:
            LP.append(lambda e_, m=prev["s2"]: e_.wait_ge(SM["cV"], m))
        LP.append(lambda e_: e_.matmul(t["pb2"][0:2, :], t["cstb"][:, CB_BLK2:CB_BLK2 + 2],
                                       b.wb(LW_SQQ, 512), start=True,
                                       stop=True).then_inc(SM["cP"], 1))
        b.inc("cP")
        mk[(tg, "s2", tb, e)] = C["cP"]
        LV.append(lambda e_, m=mk[(tg, "s2", tb, e)]: e_.wait_ge(SM["cP"], m))
        LV.append(lambda e_: e_.tensor_scalar(out=b.wf(LW_RN_F, 512)[0:2, :],
                                              in0=t["pb2"][0:2, :], scalar1=1e-24,
                                              scalar2=None, op0=ALU.max).then_inc(SM["cV"], 1))
        b.inc("cV")
        mk[(tg, "s2c", tb, e)] = C["cV"]
        prev["s2"] = C["cV"]
        LA.append(lambda e_, m=mk[(tg, "s2c", tb, e)]: e_.wait_ge(SM["cV"], m))
        LA.append(lambda e_: e_.activation(out=b.wf(LW_RN_F, 512)[0:2, :],
                                           in_=b.wf(LW_RN_F, 512)[0:2, :], func=AF.Ln))
        LA.append(lambda e_: e_.drain())
        LA.append(lambda e_: e_.activation(out=b.wf(LW_RN_F, 512)[0:2, :],
                                           in_=b.wf(LW_RN_F, 512)[0:2, :], func=AF.Exp,
                                           scale=-0.5).then_inc(SM["cA"], 1))
        b.inc("cA")
        mk[(tg, "rn", tb, e)] = C["cA"]
        LV.append(lambda e_, m=mk[(tg, "rn", tb, e)]: e_.wait_ge(SM["cA"], m))
        LV.append(lambda e_: e_.tensor_copy(out=b.wb(LW_RNB, 512)[0:2, :],
                                            in_=b.wf(LW_RN_F, 512)[0:2, :]).then_inc(SM["cV"], 1))
        b.inc("cV")
        mk[(tg, "rnbc", tb, e)] = C["cV"]
        LP.append(lambda e_, m=mk[(tg, "rnbc", tb, e)]: e_.wait_ge(SM["cV"], m))
        LP.append(lambda e_: e_.matmul(t["pw"][:, 0:512],
                                       t["cstb"][0:2, CB_SEL2:CB_SEL2 + 128],
                                       b.wb(LW_RNB, 512)[0:2, :],
                                       start=True, stop=True))
        LP.append(lambda e_: e_.matmul(t["pw"][:, 512:1024],
                                       t["cstb"][:, CB_PERM:CB_PERM + 128],
                                       b.wb(LW_QRW, 512), start=True,
                                       stop=True).then_inc(SM["cP"], 1))
        b.inc("cP")
        mk[(tg, "rnbm", tb, e)] = C["cP"]
        abo = 0 if is_q else 1024
        LV.append(lambda e_, m=mk[(tg, "rnbm", tb, e)]: e_.wait_ge(SM["cP"], m))
        LV.append(lambda e_, m=mk[(tg, "abt", tb)]: e_.wait_ge(SM["cA"], m))
        LV.append(lambda e_, a=abo: e_.tensor_tensor(
            out=b.wb(LW_PST, 512), in0=b.wb(LW_QRW, 512),
            in1=b.wb(LW_AB + a, 512), op=ALU.mult))
        LV.append(lambda e_, a=abo: e_.tensor_tensor(
            out=b.wb(LW_PST + 512, 512), in0=t["pw"][:, 512:1024],
            in1=b.wb(LW_AB + a + 512, 512), op=ALU.mult))
        LV.append(lambda e_: e_.drain())
        LV.append(lambda e_: e_.tensor_tensor(
            out=b.wb(LW_PST, 512), in0=b.wb(LW_PST, 512),
            in1=b.wb(LW_PST + 512, 512), op=ALU.add))
        LV.append(lambda e_: e_.drain())
        if is_q:
            dst = b.wb(LW_QF + hp * 2048 + (tb - 1) * 512, 512)
        else:
            dst = b.wb(LW_KF + hp * 2560 + tb * 512, 512)
        LV.append(lambda e_, d=dst: e_.tensor_tensor(
            out=d, in0=b.wb(LW_PST, 512),
            in1=t["pw"][:, 0:512], op=ALU.mult).then_inc(SM["cV"], 1))
        b.inc("cV")
        mk[(tg, "prep", tb, e)] = C["cV"]
        prev["chain"] = C["cV"]

    def vprep(tb, e, bank):
        vch = e - 8
        LA.append(lambda e_, m=mk[(tg, "qkvm", tb, e)]: e_.wait_ge(SM["cP"], m))
        if (tg, "vtpd", tb, vch - 1) in mk:
            LA.append(lambda e_, m=mk[(tg, "vtpd", tb, vch - 1)]: e_.wait_ge(SM["cP"], m))
        elif (tg, "vtpd", tb - 1, 3) in mk:
            LA.append(lambda e_, m=mk[(tg, "vtpd", tb - 1, 3)]: e_.wait_ge(SM["cP"], m))
        LA.append(lambda e_, bk=bank: e_.activation(out=b.wb(LW_VST, 512), in_=t[bk][:],
                                                    func=AF.Copy).then_inc(SM["cA"], 1))
        b.inc("cA")
        mk[(tg, "vev", tb, e)] = C["cA"]
        prev["bank"][bank] = C["cA"]
        LP.append(lambda e_, m=mk[(tg, "vev", tb, e)]: e_.wait_ge(SM["cA"], m))
        if (tg, "vstr", tb, vch - 1) in mk:
            LP.append(lambda e_, m=mk[(tg, "vstr", tb, vch - 1)]: e_.wait_ge(SM["cV"], m))
        elif (tg, "vstr", tb - 1, 3) in mk:
            LP.append(lambda e_, m=mk[(tg, "vstr", tb - 1, 3)]: e_.wait_ge(SM["cV"], m))
        for cc in range(4):
            LP.append(lambda e_, c4=cc: e_.transpose(
                t["pt"][:, c4 * 128:(c4 + 1) * 128],
                b.wb(LW_VST + c4 * 128, 128),
                t["cstb"][:, CB_ID:CB_ID + 128]))
        b.pinc(LP)
        b.inc("cP")
        mk[(tg, "vtpd", tb, vch)] = C["cP"]
        LV.append(lambda e_, m=mk[(tg, "vtpd", tb, vch)]: e_.wait_ge(SM["cP"], m))
        if tb == 0 and vch == 0:
            LV.append(lambda e_, m=mk[(tg, "vones")]: e_.wait_ge(SM["cV"], m))
        for cc in range(4):
            vc = tb * 4 + cc
            dstap = bass.AP(t["WS"], LW_VPR + (vc * 4 + vch) * 130,
                            [[55040, 128], [65, 2], [1, 64]])
            srcap = bass.AP(t["pt"], cc * 128, [[1024, 128], [64, 2], [1, 64]])
            LV.append(lambda e_, d=dstap, s_=srcap: e_.tensor_copy(out=d, in_=s_))
        b.dnop(LV, "cV")
        b.inc("cV")
        mk[(tg, "vstr", tb, vch)] = C["cV"]

    gpar = 0
    for tb in range(5):
        ycast(tb)
        abprep(tb)
        for e in list(range(4, 8)) + list(range(8, 12)) + (list(range(0, 4)) if tb >= 1 else []):
            bank = "pb0" if gpar % 2 == 0 else "pb1"
            wq_load(tb, e, gpar % 2)
            qkv_mm(tb, e, gpar % 2, bank)
            if e >= 8:
                vprep(tb, e, bank)
            else:
                qkprep(tb, e, bank, e < 4)
            gpar += 1
        mk[(tg, "ycfree", tb % 2)] = C["cP"]
        mk[(tg, "abused", tb)] = C["cV"]
    mk[(tg, "qkvdone")] = C["cV"]

    if l == 0 and ("kf" in b.dbg or "qf" in b.dbg):
        LG.append(lambda e, m=mk[(tg, "qkvdone")]: e.wait_ge(SM["cV"], m))
        for c in range(4):
            if "kf" in b.dbg:
                LG.append(lambda e, cc=c: e.dma_start(
                    out=dbg_out["kf"][cc],
                    in_=b.wb(LW_KF + cc * 2560, 2560)).then_inc(SM["sG"], 16))
                b.inc("sG", 16)
            if "qf" in b.dbg:
                LG.append(lambda e, cc=c: e.dma_start(
                    out=dbg_out["qf"][cc],
                    in_=b.wb(LW_QF + cc * 2048, 2048)).then_inc(SM["sG"], 16))
                b.inc("sG", 16)

    for w in range(4):
        if w == 1:
            LS.append(lambda e, m=mk[(tg, "mdone0")]: e.wait_ge(SM["cV"], m))
            LS.append(lambda e: e.dma_start(out=b.wb(LW_MSK, 2560),
                                            in_=t["mgen"]).then_inc(SM["sW"], 16))
            b.inc("sW", 16)
            mk[(tg, "mgen")] = C["sW"]
        for hp in range(4):
            for j, kc in enumerate(PV_ORDER):
                par = j % 2
                wd = TRIM_W[kc]
                q0 = TRIM_Q0[kc]
                if (tg, "lexp", w, hp, j - 2) in mk:
                    LP.append(lambda e, m=mk[(tg, "lexp", w, hp, j - 2)]: e.wait_ge(SM["cA"], m))
                elif (tg, "lexp", w, hp - 1, 6 + j) in mk:
                    LP.append(lambda e, m=mk[(tg, "lexp", w, hp - 1, 6 + j)]: e.wait_ge(SM["cA"], m))
                elif (tg, "lexp", w - 1, 3, 6 + j) in mk:
                    LP.append(lambda e, m=mk[(tg, "lexp", w - 1, 3, 6 + j)]: e.wait_ge(SM["cA"], m))
                if j == 0:
                    LP.append(lambda e, m=mk[(tg, "qkvdone")]: e.wait_ge(SM["cV"], m))
                    if hp == 0 and (tg, "woevall", w - 1, 1) in mk:
                        LP.append(lambda e, m=mk[(tg, "woevall", w - 1, 1)]: e.wait_ge(SM["cV"], m))
                    okey = (tg, "oev", w, hp - 1) if hp >= 1 else ((tg, "oev", w - 1, 3) if w >= 1 else None)
                    if okey and okey in mk:
                        LP.append(lambda e, m=mk[okey]: e.wait_ge(SM["cV"], m))
                kcol = w * 512 + kc * 128
                LP.append(lambda e, p=par, h=hp, kl=kcol, w_=wd, q0_=q0, w2=w:
                          e.matmul(t["pw"][:, p * 1024:p * 1024 + w_],
                                   b.wb(LW_KF + h * 2560 + kl, 128)[0:64, :],
                                   b.wb(LW_QF + h * 2048 + w2 * 512 + q0_, w_)[0:64, :],
                                   start=True, stop=True, tile_position=(0, 0)))
                LP.append(lambda e, p=par, h=hp, kl=kcol, w_=wd, q0_=q0, w2=w:
                          e.matmul(t["pw"][:, p * 1024 + 512:p * 1024 + 512 + w_],
                                   b.wb(LW_KF + h * 2560 + kl, 128)[64:128, :],
                                   b.wb(LW_QF + h * 2048 + w2 * 512 + q0_, w_)[64:128, :],
                                   start=True, stop=True,
                                   tile_position=(64, 0)).then_inc(SM["cP"], 1))
                b.inc("cP")
                mk[(tg, "lsim", w, hp, j)] = C["cP"]
                LA.append(lambda e, m=mk[(tg, "lsim", w, hp, j)]: e.wait_ge(SM["cP"], m))
                pvk = None
                if (tg, "lpv", w, hp, j - 2) in mk:
                    pvk = (tg, "lpv", w, hp, j - 2)
                elif (tg, "lpv", w, hp - 1, 6 + j) in mk:
                    pvk = (tg, "lpv", w, hp - 1, 6 + j)
                elif (tg, "lpv", w - 1, 3, 6 + j) in mk:
                    pvk = (tg, "lpv", w - 1, 3, 6 + j)
                if pvk:
                    LA.append(lambda e, m=mk[pvk]: e.wait_ge(SM["cP"], m))
                outap = bass.AP(t["WS"], LW_PST + par * 1024, [[55040, 128], [512, 2], [1, wd]])
                inap = bass.AP(t["pw"], par * 1024, [[2048, 128], [512, 2], [1, wd]])
                LA.append(lambda e, o=outap, i_=inap: e.activation(
                    out=o, in_=i_, func=AF.Exp).then_inc(SM["cA"], 1))
                b.inc("cA")
                mk[(tg, "lexp", w, hp, j)] = C["cA"]
                LV.append(lambda e, m=mk[(tg, "lexp", w, hp, j)]: e.wait_ge(SM["cA"], m))
                if w == 1 and hp == 0 and j == 0:
                    LV.append(lambda e, m=mk[(tg, "mgen")]: e.wait_ge(SM["sW"], m))
                moff = LW_MSK + TRIM_OFF[kc]
                for hh in range(2):
                    LV.append(lambda e, o=LW_PST + par * 1024 + hh * 512, mo=moff, w_=wd:
                              e.tensor_tensor(out=b.wb(o, w_), in0=b.wb(o, w_),
                                              in1=b.wb(mo, w_), op=ALU.mult))
                b.dnop(LV, "cV")
                b.inc("cV")
                mk[(tg, "lmask", w, hp, j)] = C["cV"]
                if w == 0 and hp == 3 and j == 7:
                    mk[(tg, "mdone0")] = C["cV"]
                LP.append(lambda e, m=mk[(tg, "lmask", w, hp, j)]: e.wait_ge(SM["cV"], m))
                vc = w * 4 + kc
                st = j == 0
                sp = j == 7
                LP.append(lambda e, v_=vc, h=hp, w_=wd, q0_=q0, st_=st, sp_=sp, p=par:
                          e.matmul(t["pb0"][0:65, q0_:q0_ + w_],
                                   b.wb(LW_VPR + (v_ * 4 + h) * 130, 65),
                                   b.wb(LW_PST + p * 1024, w_), start=st_, stop=sp_))
                LP.append(lambda e, v_=vc, h=hp, w_=wd, q0_=q0, st_=st, sp_=sp, p=par:
                          e.matmul(t["pb1"][0:65, q0_:q0_ + w_],
                                   b.wb(LW_VPR + (v_ * 4 + h) * 130 + 65, 65),
                                   b.wb(LW_PST + p * 1024 + 512, w_), start=st_,
                                   stop=sp_).then_inc(SM["cP"], 1))
                b.inc("cP")
                mk[(tg, "lpv", w, hp, j)] = C["cP"]
            LV.append(lambda e, m=mk[(tg, "lpv", w, hp, 7)]: e.wait_ge(SM["cP"], m))
            LV.append(lambda e: e.reciprocal(out=b.wf(LW_RN_F, 512)[64:65, :],
                                             in_=t["pb0"][64:65, :]))
            LV.append(lambda e: e.reciprocal(out=b.wf(LW_RN_F + 512, 512)[64:65, :],
                                             in_=t["pb1"][64:65, :]))
            LV.append(lambda e: e.drain())
            LV.append(lambda e: e.tensor_copy(out=b.wb(LW_RNB, 512)[64:65, :],
                                              in_=b.wf(LW_RN_F, 512)[64:65, :]))
            LV.append(lambda e: e.tensor_copy(out=b.wb(LW_RZB, 512)[64:65, :],
                                              in_=b.wf(LW_RN_F + 512, 512)[64:65, :]).then_inc(SM["cV"], 1))
            b.inc("cV")
            mk[(tg, "lrz", w, hp)] = C["cV"]
            LP.append(lambda e, m=mk[(tg, "lrz", w, hp)]: e.wait_ge(SM["cV"], m))
            LP.append(lambda e: e.matmul(t["pw"][0:64, 0:512],
                                         t["cstb"][64:65, CB_ONES:CB_ONES + 64],
                                         b.wb(LW_RNB, 512)[64:65, :],
                                         start=True, stop=True, tile_position=(64, 0)))
            LP.append(lambda e: e.matmul(t["pw"][0:64, 512:1024],
                                         t["cstb"][64:65, CB_ONES:CB_ONES + 64],
                                         b.wb(LW_RZB, 512)[64:65, :],
                                         start=True, stop=True,
                                         tile_position=(64, 0)).then_inc(SM["cP"], 1))
            b.inc("cP")
            mk[(tg, "lrzb", w, hp)] = C["cP"]
            LA.append(lambda e, m=mk[(tg, "lrzb", w, hp)]: e.wait_ge(SM["cP"], m))
            LA.append(lambda e: e.activation(out=b.wb(LW_RNB, 512)[0:64, :],
                                             in_=t["pw"][0:64, 0:512], func=AF.Copy))
            LA.append(lambda e: e.activation(out=b.wb(LW_RZB, 512)[0:64, :],
                                             in_=t["pw"][0:64, 512:1024],
                                             func=AF.Copy).then_inc(SM["cA"], 1))
            b.inc("cA")
            mk[(tg, "lrzbe", w, hp)] = C["cA"]
            LV.append(lambda e, m=mk[(tg, "lrzbe", w, hp)]: e.wait_ge(SM["cA"], m))
            if (tg, "obmv", w, hp - 1) in mk:
                LV.append(lambda e, m=mk[(tg, "obmv", w, hp - 1)]: e.wait_ge(SM["sZ"], m))
            LV.append(lambda e, h=hp: e.tensor_tensor(
                out=b.wb(LW_OSB + h * 512, 512)[0:64, :], in0=t["pb0"][0:64, :],
                in1=b.wb(LW_RNB, 512)[0:64, :], op=ALU.mult))
            LV.append(lambda e: e.tensor_tensor(
                out=b.wb(LW_VST, 512)[0:64, :], in0=t["pb1"][0:64, :],
                in1=b.wb(LW_RZB, 512)[0:64, :], op=ALU.mult).then_inc(SM["cV"], 1))
            b.inc("cV")
            mk[(tg, "oev", w, hp)] = C["cV"]
            LG.append(lambda e, m=mk[(tg, "oev", w, hp)]: e.wait_ge(SM["cV"], m))
            LG.append(lambda e, h=hp: e.dma_start(
                out=b.wb(LW_OSB + h * 512, 512)[64:128, :],
                in_=b.wb(LW_VST, 512)[0:64, :]).then_inc(SM["sZ"], 16))
            b.inc("sZ", 16)
            mk[(tg, "obmv", w, hp)] = C["sZ"]
        for ps in range(2):
            for i in range(4):
                if i == 0 and ps == 0:
                    LS.append(lambda e, m=mk[(tg, "obmv", w, 3)]: e.wait_ge(SM["sZ"], m))
                wkey = (tg, "wost", w, ps, i - 2) if i >= 2 else ((tg, "wost", w, ps - 1, i + 2) if ps >= 1 else ((tg, "wost", w - 1, 1, i + 2) if w >= 1 else None))
                if wkey and wkey in mk:
                    LS.append(lambda e, m=mk[wkey]: e.wait_ge(SM["cP"], m))
                LS.append(lambda e, ii=i: e.dma_start(
                    out=b.wb(LW_WOS_N + (ii % 2) * 1024, 1024),
                    in_=t["wo_bf"][l * 512 + ii * 128:l * 512 + (ii + 1) * 128, :]).then_inc(SM["sW"], 16))
                b.inc("sW", 16)
                mk[(tg, "wol", w, ps, i)] = C["sW"]
                LP.append(lambda e, m=mk[(tg, "wol", w, ps, i)]: e.wait_ge(SM["sW"], m))
                if i == 0 and ps == 0:
                    LP.append(lambda e, m=mk[(tg, "obmv", w, 3)]: e.wait_ge(SM["sZ"], m))
                if i == 0 and (tg, "woevall", w, ps - 1) in mk:
                    LP.append(lambda e, m=mk[(tg, "woevall", w, ps - 1)]: e.wait_ge(SM["cV"], m))
                if i == 0 and ps == 0 and (tg, "woevall", w - 1, 1) in mk:
                    LP.append(lambda e, m=mk[(tg, "woevall", w - 1, 1)]: e.wait_ge(SM["cV"], m))
                for j in range(4):
                    dd = ps * 4 + j
                    LP.append(lambda e, ii=i, dd_=dd, jj=j: e.matmul(
                        t["pw"][:, jj * 512:(jj + 1) * 512],
                        b.wb(LW_WOS_N + (ii % 2) * 1024 + dd_ * 128, 128),
                        b.wb(LW_OSB + ii * 512, 512),
                        start=(ii == 0), stop=(ii == 3)))
                b.pinc(LP)
                b.inc("cP")
                mk[(tg, "wost", w, ps, i)] = C["cP"]
            for j in range(4):
                dd = ps * 4 + j
                LV.append(lambda e, m=mk[(tg, "wost", w, ps, 3)]: e.wait_ge(SM["cP"], m))
                LV.append(lambda e, dd_=dd, jj=j, w2=w: e.tensor_tensor(
                    out=b.Rown(dd_)[:, w2 * 512:(w2 + 1) * 512],
                    in0=b.Rown(dd_)[:, w2 * 512:(w2 + 1) * 512],
                    in1=t["pw"][:, jj * 512:(jj + 1) * 512], op=ALU.add).then_inc(SM["cV"], 1))
                b.inc("cV")
            mk[(tg, "woevall", w, ps)] = C["cV"]
    mk[(tg, "x2mid")] = C["cV"]

    shc = t["shC1"] if l == 0 else t["shC2"]
    ep = 2 + 2 * l
    zsem = "sZ2" if l == 0 else "sZ4"
    cxf = LW_CX // 2
    LV.append(lambda e, m=mk[(tg, "x2mid")]: e.wait_ge(SM["cV"], m))
    for c in range(8):
        LV.append(lambda e, cc=c: e.tensor_copy(out=b.wf(cxf, 16)[:, cc:cc + 1],
                                                in_=b.Rcol(cc, 2 + TOWN - 1, 1)))
    b.dnop(LV, "cV")
    b.inc("cV")
    mk[(tg, "cxs")] = C["cV"]
    LG.append(lambda e, m=mk[(tg, "cxs")]: e.wait_ge(SM["cV"], m))
    dstap = bass.AP(shc.tensor, 0, [[NCH, 2 * 128], [1, NCH]])
    LG.append(lambda e, d=dstap: e.indirect_dma_start(
        d, bass.IndirectOffsetOnAxis(t["idxs"][:, 144:145], 0),
        b.wf(cxf, 16)[:, 0:8], None).then_inc(SM["sG"], 16))
    b.inc("sG", 16)
    emit_flag_write(LG, b, ep, C["sG"])
    emit_poll(LG, b, ep, zsem)
    srcap = bass.AP(shc.tensor, 0, [[NCH, 2 * 128], [1, NCH]])
    LG.append(lambda e, s_=srcap: e.indirect_dma_start(
        b.wf(cxf, 16)[:, 8:16], None, s_,
        bass.IndirectOffsetOnAxis(t["idxs"][:, 145:146], 0)).then_inc(SM["sG"], 16))
    b.inc("sG", 16)
    mk[(tg, "cxg")] = C["sG"]
    LV.append(lambda e, m=mk[(tg, "cxg")]: e.wait_ge(SM["sG"], m))
    LV.append(lambda e: e.tensor_scalar(out=b.wf(cxf, 16)[:, 8:16],
                                        in0=b.wf(cxf, 16)[:, 8:16],
                                        scalar1=t["cst"][:, CS_KEEP:CS_KEEP + 1],
                                        scalar2=None, op0=ALU.mult))
    LV.append(lambda e: e.drain())
    for c in range(8):
        LV.append(lambda e, cc=c: e.tensor_copy(out=b.Rm1(cc),
                                                in_=b.wf(cxf, 16)[:, 8 + cc:9 + cc]))
    b.dnop(LV, "cV")
    b.inc("cV")
    mk[(tg, "cx")] = C["cV"]

    if "x2mid" in b.dbg and l == 0:
        LG.append(lambda e, m=mk[(tg, "x2mid")]: e.wait_ge(SM["cV"], m))
        for c in range(8):
            LG.append(lambda e, cc=c: e.dma_start(
                out=dbg_out["x2mid"][cc], in_=b.Rown(cc)).then_inc(SM["sG"], 16))
        b.inc("sG", 8 * 16)

    for c in range(8):
        src = b.Rown(c) if c < 4 else b.Rsh(c)
        LA.append(lambda e, m=mk[(tg, "cx")]: e.wait_ge(SM["cV"], m))
        if c >= 1:
            LA.append(lambda e, m=mk[(tg + "ln2", "smm", c - 1)]: e.wait_ge(SM["cP"], m))
        LA.append(lambda e, s_=src: e.activation(out=b.wb(L2_CAST, 2048), in_=s_,
                                                 func=AF.Copy))
        LA.append(lambda e, s_=src: e.activation(out=b.wb(L2_CAST + 2048, 2048), in_=s_,
                                                 func=AF.Square).then_inc(SM["cA"], 1))
        b.inc("cA")
        mk[(tg, "ln2c", c)] = C["cA"]
        if c == 0:
            LP.append(lambda e, m=mk[(tg, "woevall", 3, 1)]: e.wait_ge(SM["cV"], m))
        emit_stats_mm2(b, LP, tg + "ln2", c, 8, b.wb(L2_CAST, 2048),
                       b.wb(L2_CAST + 2048, 2048), [512] * 4, wait_key=(tg, "ln2c", c))
    emit_rowmath_ab(b, LP, LA, LV, LG, 2048, D, tg + "ln2",
                    R0F=L2_R_F, R1F=L2_R_F + 2048, R2F=L2_R_F + 4096,
                    R3F=L2_CS3_F, ABBF=L2_ABB_F, BBBF=L2_BBB_F, RBB=L2_RB_B)
    for c in range(8):
        src = b.Rown(c) if c < 4 else b.Rsh(c)
        LV.append(lambda e, m=mk[(tg + "ln2", "ab")]: e.wait_ge(SM["cA"], m))
        if (tg, "xnc", c - 1) in mk:
            LV.append(lambda e, m=mk[(tg, "xnc", c - 1)]: e.wait_ge(SM["cA"], m))
        LV.append(lambda e, s_=src: e.tensor_tensor(
            out=b.wf(L2_CS3_F, 2048), in0=s_, in1=b.wf(L2_ABB_F, 2048), op=ALU.mult))
        LV.append(lambda e: e.drain())
        LV.append(lambda e: e.tensor_tensor(
            out=b.wf(L2_CS3_F, 2048), in0=b.wf(L2_CS3_F, 2048),
            in1=b.wf(L2_BBB_F, 2048), op=ALU.add).then_inc(SM["cV"], 1))
        b.inc("cV")
        mk[(tg, "xnt", c)] = C["cV"]
        LA.append(lambda e, m=mk[(tg, "xnt", c)]: e.wait_ge(SM["cV"], m))
        LA.append(lambda e, cc=c: e.activation(
            out=b.wb(L2_XN + cc * 2048, 2048), in_=b.wf(L2_CS3_F, 2048), func=AF.Identity,
            scale=t["cst"][:, CS_LNG + l * NCH + cc:CS_LNG + l * NCH + cc + 1],
            bias=t["cst"][:, CS_LNB + l * NCH + cc:CS_LNB + l * NCH + cc + 1]).then_inc(SM["cA"], 1))
        b.inc("cA")
        mk[(tg, "xnc", c)] = C["cA"]
    mk[(tg, "xn")] = C["cA"]

    W1G = [(0, 6), (6, 6), (12, 6), (18, 4)]
    for ht in range(8):
        xcol = ht * 256
        for half in range(2):
            for gi, (g0, gn) in enumerate(W1G):
                for dd in range(8):
                    par = dd % 2
                    wkey = (tg, "w1l", ht, half, gi, dd - 2)
                    if wkey in mk:
                        LS.append(lambda e, m=mk[wkey]: e.wait_ge(SM["cP"], m))
                    else:
                        if gi >= 1:
                            pkey = (tg, "w1lx", ht, half, gi - 1)
                        elif half >= 1:
                            pkey = (tg, "w1lx", ht, 0, len(W1G) - 1)
                        elif ht >= 1:
                            pkey = (tg, "w1lx", ht - 1, 1, len(W1G) - 1)
                        else:
                            pkey = None
                        if pkey is not None and pkey in mk:
                            LS.append(lambda e, m=mk[pkey]: e.wait_ge(SM["cP"], m))
                        elif ht == 0:
                            LS.append(lambda e, m=mk[(tg + "ln2", "rows")]: e.wait_ge(SM["cV"], m))
                    LS.append(lambda e, dd_=dd, h_=half, g_=g0, n_=gn, p=par: e.dma_start(
                        out=b.wb(LH_W1S + p * 768, n_ * 128),
                        in_=t["w1_bf"][l * D + dd_ * 128:l * D + (dd_ + 1) * 128,
                                       h_ * HIDP + g_ * 128:h_ * HIDP + (g_ + n_) * 128]).then_inc(SM["sW"], 16))
                    b.inc("sW", 16)
                    mk[(tg, "w1ld", ht, half, gi, dd)] = C["sW"]
                    LP.append(lambda e, m=mk[(tg, "w1ld", ht, half, gi, dd)]: e.wait_ge(SM["sW"], m))
                    if dd == 0:
                        LP.append(lambda e, m=mk[(tg, "xn")]: e.wait_ge(SM["cA"], m))
                        ekey = (tg, "w1ev", ht, half, gi - 1) if gi >= 1 else                                ((tg, "w1ev", ht, half - 1, 3) if half >= 1 else
                                ((tg, "w2done", ht - 1) if ht >= 1 else None))
                        if ekey and ekey in mk:
                            LP.append(lambda e, m=mk[ekey]: e.wait_ge(SM["cA"], m))
                    for j in range(gn):
                        bank = ["pw0", "pw1", "pw2", "pw3", "pb0", "pb1"][j]
                        if bank.startswith("pw"):
                            out = t["pw"][:, int(bank[2]) * 512:int(bank[2]) * 512 + 256]
                        else:
                            out = t[bank][:, 0:256]
                        LP.append(lambda e, o=out, jj=j, dd_=dd, p=par, xc=xcol: e.matmul(
                            o, b.wb(LH_W1S + p * 768 + jj * 128, 128),
                            b.wb(L2_XN + dd_ * 2048 + xc, 256),
                            start=(dd_ == 0), stop=(dd_ == 7)))
                    b.pinc(LP)
                    b.inc("cP")
                    mk[(tg, "w1l", ht, half, gi, dd)] = C["cP"]
                mk[(tg, "w1lx", ht, half, gi)] = C["cP"]
                LA.append(lambda e, m=mk[(tg, "w1l", ht, half, gi, 7)]: e.wait_ge(SM["cP"], m))
                for j in range(gn):
                    hcg = half * NAC + g0 + j
                    bank = ["pw0", "pw1", "pw2", "pw3", "pb0", "pb1"][j]
                    if bank.startswith("pw"):
                        inp = t["pw"][:, int(bank[2]) * 512:int(bank[2]) * 512 + 256]
                    else:
                        inp = t[bank][:, 0:256]
                    fn = AF.Gelu if half == 1 else AF.Identity
                    LA.append(lambda e, i_=inp, hcg_=hcg, f=fn: e.activation(
                        out=b.wb(LH_HH + hcg_ * 256, 256), in_=i_, func=f,
                        bias=t["cst"][:, CS_B1 + l * 2 * NAC + hcg_:CS_B1 + l * 2 * NAC + hcg_ + 1]))
                b.dnop(LA, "cA")
                b.inc("cA")
                mk[(tg, "w1ev", ht, half, gi)] = C["cA"]
        LV.append(lambda e, m=mk[(tg, "w1ev", ht, 1, 3)]: e.wait_ge(SM["cA"], m))
        for cg in range(NAC):
            LV.append(lambda e, c_=cg: e.tensor_tensor(
                out=b.wb(LH_HH + c_ * 256, 256), in0=b.wb(LH_HH + c_ * 256, 256),
                in1=b.wb(LH_HH + (NAC + c_) * 256, 256), op=ALU.mult))
        b.dnop(LV, "cV")
        b.inc("cV")
        mk[(tg, "gg", ht)] = C["cV"]
        if "hh0" in b.dbg and l == 0 and ht == 0:
            LG.append(lambda e, m=mk[(tg, "gg", 0)]: e.wait_ge(SM["cV"], m))
            LG.append(lambda e: e.dma_start(out=dbg_out["hh0"][0],
                                            in_=b.wb(LH_HH, 11264)).then_inc(SM["sG"], 16))
            b.inc("sG", 16)
        for ps in range(2):
            for hc in range(NAC):
                par = hc % 2
                wkey = (tg, "w2m", ht, ps, hc - 2) if hc >= 2 else                        ((tg, "w2m", ht, ps - 1, NAC - 2 + hc) if ps >= 1 else
                        ((tg, "w2m", ht - 1, 1, NAC - 2 + hc) if ht >= 1 else None))
                if wkey and wkey in mk:
                    LS.append(lambda e, m=mk[wkey]: e.wait_ge(SM["cP"], m))
                elif ht == 0:
                    LS.append(lambda e, m=mk[(tg + "ln2", "rows")]: e.wait_ge(SM["cV"], m))
                LS.append(lambda e, hc_=hc, p=par: e.dma_start(
                    out=b.wb(LH_W2S + p * 1024, 1024),
                    in_=t["w2_bf"][l * HIDP + hc_ * 128:l * HIDP + (hc_ + 1) * 128, :]).then_inc(SM["sW"], 16))
                b.inc("sW", 16)
                mk[(tg, "w2ld", ht, ps, hc)] = C["sW"]
                LP.append(lambda e, m=mk[(tg, "w2ld", ht, ps, hc)]: e.wait_ge(SM["sW"], m))
                if hc == 0:
                    LP.append(lambda e, m=mk[(tg, "gg", ht)]: e.wait_ge(SM["cV"], m))
                    ekey = (tg, "w2ev", ht, ps - 1) if ps >= 1 else ((tg, "w2ev", ht - 1, 1) if ht >= 1 else None)
                    if ekey and ekey in mk:
                        LP.append(lambda e, m=mk[ekey]: e.wait_ge(SM["cV"], m))
                for j in range(4):
                    dout = ps * 4 + j
                    LP.append(lambda e, jj=j, hc_=hc, p=par, do=dout: e.matmul(
                        t["pw"][:, jj * 512:jj * 512 + 256],
                        b.wb(LH_W2S + p * 1024 + do * 128, 128),
                        b.wb(LH_HH + hc_ * 256, 256),
                        start=(hc_ == 0), stop=(hc_ == NAC - 1)))
                b.pinc(LP)
                b.inc("cP")
                mk[(tg, "w2m", ht, ps, hc)] = C["cP"]
            LA.append(lambda e, m=mk[(tg, "w2m", ht, ps, NAC - 1)]: e.wait_ge(SM["cP"], m))
            for j in range(4):
                dout = ps * 4 + j
                LA.append(lambda e, jj=j, do=dout: e.activation(
                    out=b.wf(LH_W2E_F + jj * 256, 256), in_=t["pw"][:, jj * 512:jj * 512 + 256],
                    func=AF.Identity,
                    bias=t["cst"][:, CS_B2 + l * NCH + do:CS_B2 + l * NCH + do + 1]))
            b.dnop(LA, "cA")
            b.inc("cA")
            mk[(tg, "w2e", ht, ps)] = C["cA"]
            LV.append(lambda e, m=mk[(tg, "w2e", ht, ps)]: e.wait_ge(SM["cA"], m))
            for j in range(4):
                dout = ps * 4 + j
                LV.append(lambda e, jj=j, do=dout, xc=xcol: e.tensor_tensor(
                    out=b.Rown(do)[:, xc:xc + 256], in0=b.Rown(do)[:, xc:xc + 256],
                    in1=b.wf(LH_W2E_F + jj * 256, 256), op=ALU.add))
            b.dnop(LV, "cV")
            b.inc("cV")
            mk[(tg, "w2ev", ht, ps)] = C["cV"]
        mk[(tg, "w2done", ht)] = C["cA"]
    mk[(tg, "done")] = C["cV"]

    for name, getter in (("xn", lambda c: b.wb(L2_XN + c * 2048, 2048)),
                         ("hh", lambda c: b.wb(LH_HH, 11264)),
                         ("x1", lambda c: b.Rown(c))):
        if name in b.dbg and (name != "x1" or l == 0) and (l == 0):
            LG.append(lambda e, m=C["cV"]: e.wait_ge(SM["cV"], m))
            nch = 1 if name == "hh" else (4 if name in ("kf", "qf") else 8)
            for c in range(nch):
                LG.append(lambda e, cc=c, g=getter, nm=name: e.dma_start(
                    out=dbg_out[nm][cc], in_=g(cc)).then_inc(SM["sG"], 16))
            b.inc("sG", nch * 16)

    block.tensor(lambda e: [f(e) for f in LP])
    block.scalar(lambda e: [f(e) for f in LA])
    block.vector(lambda e: [f(e) for f in LV])
    block.gpsimd(lambda e: [f(e) for f in LG])
    block.sync(lambda e: [f(e) for f in LS])


def _host_consts(core):
    b_, h_ = core // 2, core % 2
    t0 = 2048 * h_
    c = {}
    ar = np.arange
    c["rel"] = (ar(VECW, dtype=np.float32) - 4223.0).reshape(1, VECW)
    p = ar(128)
    selh8 = (p[:, None] // 16 == ar(8)[None, :]).astype(np.float32)
    c["selh"] = np.concatenate([selh8, selh8], axis=1).astype(BF)
    c["ident_bf"] = np.eye(128, dtype=np.float32).astype(BF)
    c["ident_f"] = np.eye(128, dtype=np.float32)
    c["ones_bf"] = np.ones((128, 128), np.float32).astype(BF)
    swap = np.where(p % 64 < 32, p + 32, p - 32)
    perm = np.zeros((128, 128), np.float32)
    perm[swap, p] = 1.0
    c["permsw"] = perm.astype(BF)
    c["blk2"] = ((p[:, None] < 64) == (ar(2)[None, :] == 0)).astype(BF)
    c["sel2"] = ((ar(128)[None, :] < 64) == (ar(2)[:, None] == 0)).astype(BF)
    idxW = np.zeros((128, 128), np.int32)
    for head in range(16):
        for qt in range(4):
            for nf in range(2):
                col = head * 8 + qt * 2 + nf
                wrap = 0 if nf == 0 else (-2048 if h_ == 0 else 2048)
                idxW[:, col] = head * VECW + 2303 + 512 * qt - p + wrap
    c["idxW"] = idxW
    j = ar(2560, dtype=np.float64)
    pos = t0 - 512 + j
    half_idx = ar(0, 64, 2, dtype=np.float64)
    inv_freq = 1.0 / 10000.0 ** (half_idx / 64.0)
    fr = pos[None, :] * inv_freq[:, None]
    fr = np.concatenate([fr, fr], 0)
    base = (half_idx + 0.4 * 64.0) / (1.4 * 64.0)
    power = (pos - 2048.0) / 256.0
    sc = base[:, None] ** power[None, :]
    sc = np.concatenate([sc, sc], 0)
    sign = np.where(ar(64) % 64 < 32, -1.0, 1.0)[:, None]
    cq = np.cos(fr) * sc * 8.0
    sq = np.sin(fr) * sc * sign * 8.0
    ck = np.cos(fr) / sc
    sk = np.sin(fr) / sc * sign
    for nm, v in (("rotcq", cq), ("rotsq", sq), ("rotck", ck), ("rotsk", sk)):
        c[nm] = np.concatenate([v, v], 0).astype(np.float32).astype(BF)
    mgen = np.zeros((128, 2560), np.float32)
    for kc in range(8):
        K = kc * 128 + p
        fi = ar(TRIM_W[kc])
        q = TRIM_Q0[kc] + fi
        m = ((q[None, :] >= K[:, None] - 512) & (q[None, :] <= K[:, None]))
        mgen[:, TRIM_OFF[kc]:TRIM_OFF[kc] + TRIM_W[kc]] = m
    mw0 = mgen.copy()
    if h_ == 0:
        mw0[:, :TRIM_OFF[4]] = 0.0
    c["mgen"] = mgen.astype(BF)
    c["mw0"] = mw0.astype(BF)
    c["keepc"] = np.full((128, 1), 0.0 if h_ == 0 else 1.0, np.float32)
    idxEx = np.zeros((128, 8), np.int32)
    ridxEx = np.zeros((128, 8), np.int32)
    for cc in range(8):
        idxEx[:, cc] = h_ * 1024 + cc * 128 + p
        ridxEx[:, cc] = (1 - h_) * 1024 + cc * 128 + p
    c["idxEx"] = idxEx
    c["ridxEx"] = ridxEx
    c["idxC"] = (h_ * 128 + p).reshape(128, 1).astype(np.int32)
    c["ridxC"] = ((1 - h_) * 128 + p).reshape(128, 1).astype(np.int32)
    c["fwidx"] = np.array([[16 * h_], [16 * h_ + 8]], np.int32)
    c["fridx"] = np.array([[16 * (1 - h_)], [16 * (1 - h_) + 8]], np.int32)
    return c


def _host_params(ins):
    p = np.arange(128)
    c = {}
    off = np.asarray(ins["L0_attn_kernel_offsets"], np.float32)
    amp = np.asarray(ins["L0_attn_kernel_amplitudes"], np.float32)
    shp = np.asarray(ins["L0_attn_kernel_sharpness"], np.float32)
    for nm, v in (("koff", off), ("kamp", amp), ("kshp", shp)):
        m = np.zeros((128, 2), np.float32)
        for ch in range(2):
            m[:, ch] = v[p // 16 + 8 * ch, p % 16]
        c[nm] = m
    c["w_in"] = np.ascontiguousarray(ins["L0_attn_w_in"], dtype=np.float32)
    c["wqkv_in"] = np.ascontiguousarray(ins["L0_ffn_wqkv"], dtype=np.float32)
    c["wo_in"] = np.ascontiguousarray(ins["L0_ffn_wo"], dtype=np.float32)
    w1 = np.asarray(ins["L0_ffn_w1"], np.float32)
    w1p = np.zeros((2, D, 2 * HIDP), np.float32)
    w1p[:, :, 0:HID] = w1[:, :, 0:HID]
    w1p[:, :, HIDP:HIDP + HID] = w1[:, :, HID:2 * HID]
    c["w1p_in"] = w1p
    w2 = np.asarray(ins["L0_ffn_w2"], np.float32)
    w2p = np.zeros((2, HIDP, D), np.float32)
    w2p[:, 0:HID, :] = w2
    c["w2p_in"] = w2p
    b1 = np.asarray(ins["L0_ffn_b1"], np.float32)
    b1pad = np.zeros((2, 2 * HIDP), np.float32)
    b1pad[:, 0:HID] = b1[:, 0:HID]
    b1pad[:, HIDP:HIDP + HID] = b1[:, HID:]
    b1p = np.zeros((128, 2 * 2 * NAC), np.float32)
    for l in range(2):
        for hcg in range(2 * NAC):
            b1p[:, l * 2 * NAC + hcg] = b1pad[l, hcg * 128 + p]
    c["b1p"] = b1p
    for nm, src in (("b2p", "L0_ffn_b2"), ("lngp", "L0_ffn_ln_g"), ("lnbp", "L0_ffn_ln_b")):
        v = np.asarray(ins[src], np.float32)
        m = np.zeros((128, 2 * NCH), np.float32)
        for l in range(2):
            for cc in range(NCH):
                m[:, l * NCH + cc] = v[l, cc * 128 + p]
        c[nm] = m
    for nm, src in (("qscp", "L0_ffn_q_scale"), ("kscp", "L0_ffn_k_scale")):
        v = np.asarray(ins[src], np.float32)
        m = np.zeros((128, 2), np.float32)
        for l in range(2):
            m[0:64, l] = v[l]
            m[64:128, l] = v[l]
        c[nm] = m
    return c


_BUILD_CACHE = {}


def kernel(L0_attn_x, L0_attn_w_in, L0_attn_kernel_offsets, L0_attn_kernel_amplitudes,
           L0_attn_kernel_sharpness, L0_ffn_wqkv, L0_ffn_q_scale, L0_ffn_k_scale,
           L0_ffn_wo, L0_ffn_ln_g, L0_ffn_ln_b, L0_ffn_w1, L0_ffn_b1, L0_ffn_w2,
           L0_ffn_b2, _stop_phase=99, _dbg=()):
    ins = dict(L0_attn_x=L0_attn_x, L0_attn_w_in=L0_attn_w_in,
               L0_attn_kernel_offsets=L0_attn_kernel_offsets,
               L0_attn_kernel_amplitudes=L0_attn_kernel_amplitudes,
               L0_attn_kernel_sharpness=L0_attn_kernel_sharpness,
               L0_ffn_wqkv=L0_ffn_wqkv, L0_ffn_q_scale=L0_ffn_q_scale,
               L0_ffn_k_scale=L0_ffn_k_scale, L0_ffn_wo=L0_ffn_wo,
               L0_ffn_ln_g=L0_ffn_ln_g, L0_ffn_ln_b=L0_ffn_ln_b,
               L0_ffn_w1=L0_ffn_w1, L0_ffn_b1=L0_ffn_b1, L0_ffn_w2=L0_ffn_w2,
               L0_ffn_b2=L0_ffn_b2)
    x = np.ascontiguousarray(np.asarray(L0_attn_x, np.float32))
    params = _host_params(ins)
    key = (_stop_phase, tuple(sorted(_dbg)))
    if key not in _BUILD_CACHE:
        _BUILD_CACHE[key] = build_nc(stop_phase=_stop_phase, dbg=_dbg)
    nc = _BUILD_CACHE[key]
    in_maps = []
    for core in range(8):
        b_, h_ = core // 2, core % 2
        t0 = 2048 * h_
        m = dict(params)
        m.update(_host_consts(core))
        m["x_own"] = np.ascontiguousarray(x[b_, t0:t0 + 2048])
        m["x_oth"] = np.ascontiguousarray(x[b_, 2048 - t0:2048 - t0 + 2048])
        in_maps.append(m)
    res = run_bass_kernel_spmd(nc, in_maps, core_ids=list(range(8)))
    out = np.zeros((B, T, D), np.float32)
    for core in range(8):
        b_, h_ = core // 2, core % 2
        out[b_, 2048 * h_:2048 * h_ + 2048] = np.asarray(res.results[core]["y_out"],
                                                         np.float32)
    if _dbg:
        kernel._last_results = res
    return out
